# version 08
# baseline (speedup 1.0000x reference)
import contextlib
import os
import numpy as np
import ml_dtypes
import concourse.bass as bass
import concourse.mybir as mybir
from concourse.bass_utils import run_bass_kernel_spmd

F32 = mybir.dt.float32
BF16 = mybir.dt.bfloat16
I32 = mybir.dt.int32
ALU = mybir.AluOpType
AF = mybir.ActivationFunctionType
AX = mybir.AxisListType.X
BF = ml_dtypes.bfloat16

STAGE = int(os.environ.get('KSTAGE', '4'))
NT, TT, SC, SCT, CH = 2048, 16, 4, 512, 64
DM, DKT, ICH, ITL = 1024, 8, 2816, 22
LNEPS = 1e-5
SUB = int(os.environ.get('KSUB', '99'))
SUBB = int(os.environ.get('KSUBB', '99'))
NOAG = int(os.environ.get('KNOAG', '0'))
KEVAC = int(os.environ.get('KEVAC', '1'))
KTRN = int(os.environ.get('KTRN', '9999'))
KSRC = int(os.environ.get('KSRC', '0'))
KPROJ = int(os.environ.get('KPROJ', '4'))
ENGS = ("sync", "gpsimd", "vector", "scalar", "tensor")


def build():
    nc = bass.Bass("TRN2", target_bir_lowering=False,
                   detect_race_conditions=False,
                   dynamic_dma_scratch_size=16384)

    def din(name, shape, dt=F32):
        return nc.dram_tensor(name, shape, dt, kind="ExternalInput").ap()

    x_in = din("x_in", [NT, DM])
    wqa = din("wqa", [DM, 512]); wka = din("wka", [DM, 512])
    wva = din("wva", [DM, 512]); waa = din("waa", [DM, 512])
    baa = din("baa", [128, 4])
    gamx = din("gamx", [128, DM]); betx = din("betx", [128, DM])
    woa = din("woa", [512, DM])
    wqb = din("wqb", [DM, DM]); wkb = din("wkb", [DM, DM])
    wvb = din("wvb", [DM, DM]); wbb = din("wbb", [DM, 8])
    lnwx = din("lnwx", [128, DM]); lnbx = din("lnbx", [128, DM])
    wg_sl = din("wg_sl", [ICH, 128]); wu_sl = din("wu_sl", [ICH, 128])
    wd_sl = din("wd_sl", [DM, 3 * 128])
    ident = din("ident", [128, 128], BF16)
    m_triu = din("m_triu", [128, 64])
    m_tril1 = din("m_tril1", [128, 64])
    mu128 = din("mu128", [128, 128])
    ml128s = din("ml128s", [128, 128])
    i64b = din("i64b", [128, 64], BF16)
    scanm = din("scanm", [128, SCT])
    ones8 = din("ones8", [128, 8])
    iconst = din("iconst", [128, 8], I32)
    pm = din("pm", [128, 8]); pmi = din("pmi", [128, 8])
    x_out = nc.dram_tensor("x_out", [NT, DM], F32, kind="ExternalOutput").ap()
    dbg_out = nc.dram_tensor("dbg_out", [128, 2048], F32, kind="ExternalOutput").ap()

    ag0_i = nc.dram_tensor("ag0_i", [128, 2 * ICH + 3 * DM], BF16)
    ag0_o = nc.dram_tensor("ag0_o", [8 * 128, 2 * ICH + 3 * DM], BF16)
    ag1_i = nc.dram_tensor("ag1_i", [128, 4 * 64 + 4], BF16)
    ag1_o = nc.dram_tensor("ag1_o", [8 * 128, 4 * 64 + 4], BF16)
    ag2_i = nc.dram_tensor("ag2_i", [128, 16 * 128], BF16)
    ag2_o = nc.dram_tensor("ag2_o", [8 * 128, 16 * 128], BF16)
    o0scr = nc.dram_tensor("o0scr", [128, DKT * NT], BF16)
    gscr = nc.dram_tensor("gscr", [128, DKT * NT], BF16)

    es = contextlib.ExitStack()
    with es:
        E = es.enter_context
        x_sb = E(nc.sbuf_tensor("x_sb", [128, TT * DM], BF16))
        s_id = E(nc.sbuf_tensor("s_id", [128, 128], BF16))
        s_mtu = E(nc.sbuf_tensor("s_mtu", [128, 64], F32))
        s_mt1 = E(nc.sbuf_tensor("s_mt1", [128, 64], F32))
        s_i64 = E(nc.sbuf_tensor("s_i64", [128, 64], BF16))
        s_scm = E(nc.sbuf_tensor("s_scm", [128, SCT], F32))
        s_on8 = E(nc.sbuf_tensor("s_on8", [128, 8], F32))
        s_ic = E(nc.sbuf_tensor("s_ic", [128, 8], I32))
        s_pm = E(nc.sbuf_tensor("s_pm", [128, 8], F32))
        s_pmi = E(nc.sbuf_tensor("s_pmi", [128, 8], F32))
        s_baa = E(nc.sbuf_tensor("s_baa", [128, 4], F32))
        st_nmu = E(nc.sbuf_tensor("st_nmu", [128, TT], F32))
        st_rst = E(nc.sbuf_tensor("st_rst", [128, TT], F32))
        st_nm2 = E(nc.sbuf_tensor("st_nm2", [128, TT], F32))
        st_rs2 = E(nc.sbuf_tensor("st_rs2", [128, TT], F32))
        st_sum = E(nc.sbuf_tensor("st_sum", [128, TT], F32))
        st_ssq = E(nc.sbuf_tensor("st_ssq", [128, TT], F32))
        st_t1 = E(nc.sbuf_tensor("st_t1", [128, TT], F32))
        st_t2 = E(nc.sbuf_tensor("st_t2", [128, TT], F32))
        st_snap = E(nc.sbuf_tensor("st_snap", [128, 6 * TT], F32))
        pb = [E(nc.psum_tensor(f"pb{i}", [128, 512], F32)) for i in range(6)]
        pt = [E(nc.psum_tensor(f"pt{i}", [128, 512], BF16)) for i in range(2)]

        sems = {nm: [E(nc.semaphore(nm)), 0]
                for nm in ("SD", "GD", "CC", "PH", "TQ", "VQ", "SQ", "GQ")}
        blk = E(nc.Block())

        QSEM = {"sync": "SD", "gpsimd_dma": "GD", "gpsimd": "GQ",
                "vector": "VQ", "scalar": "SQ", "tensor": "TQ"}
        BUDGET = {"TQ": int(os.environ.get("KBT", "24")), "VQ": int(os.environ.get("KBV", "1")), "SQ": int(os.environ.get("KBS", "12")), "GQ": int(os.environ.get("KBG", "3"))}
        GATED = {
            "tensor_tensor", "tensor_scalar", "tensor_copy", "tensor_reduce",
            "reduce_max", "reduce_sum", "memset", "memzero", "reciprocal",
            "tensor_tensor_scan", "scalar_tensor_tensor", "select", "iota",
            "activation", "copy", "add", "mul", "sign", "square",
            "tensor_add", "tensor_sub", "tensor_mul", "tensor_max",
            "tensor_scalar_add", "tensor_scalar_sub", "tensor_scalar_mul",
            "tensor_scalar_max", "tensor_scalar_min", "tensor_single_scalar",
            "matmul", "transpose",
        }
        nissued = {k: [0] for k in BUDGET}
        emitted = {k: [0] for k in BUDGET}

        class EWrap:
            def __init__(self, e, qn):
                self._e = e
                self._qn = qn

            def __getattr__(self, name):
                f = getattr(self._e, name)
                if name not in GATED:
                    return f
                qn = self._qn
                sem = sems[qn][0]
                cnt = emitted[qn]
                bud = BUDGET[qn]

                def g(*a, **kw):
                    if cnt[0] >= bud:
                        self._e.wait_ge(sem, cnt[0] - bud + 1)
                    inst = f(*a, **kw)
                    inst.then_inc(sem, 1)
                    cnt[0] += 1
                    return inst
                return g

        class _DummyInst:
            def then_inc(self, *a, **kw):
                return self

            def branch_hint(self, *a, **kw):
                return self

        _DUMMY = _DummyInst()

        class CountStub:
            def __init__(self, box):
                self._box = box

            def __getattr__(self, name):
                if name in GATED:
                    box = self._box

                    def g(*a, **kw):
                        box[0] += 1
                        return _DUMMY
                    return g
                return lambda *a, **kw: _DUMMY

        PROG = {e: [] for e in ("sync", "gpsimd", "vector", "scalar",
                                "tensor")}

        def flush_prog():
            for beng in PROG:
                fns = PROG[beng]
                if not fns:
                    continue
                PROG[beng] = []

                def mega(e, fns=fns):
                    for fn in fns:
                        fn(e)
                getattr(blk, beng)(mega)

        def step(eng, deps, f, n_dma=0):
            qn = QSEM[eng]
            beng = "gpsimd" if eng == "gpsimd_dma" else eng
            gated_engine = qn in BUDGET and eng != "gpsimd_dma"

            def _(e):
                seen = {}
                for dn, dv in deps:
                    if dv > seen.get(dn, 0):
                        seen[dn] = dv
                for dn, dv in seen.items():
                    e.wait_ge(sems[dn][0], dv)
                if gated_engine:
                    n0 = emitted[qn][0]
                    f(EWrap(e, qn))
                    if emitted[qn][0] == n0:
                        e.nop().then_inc(sems[qn][0], 1)
                        emitted[qn][0] += 1
                    return
                f(e)
            PROG[beng].append(_)
            if gated_engine:
                box = [0]
                f(CountStub(box))
                nissued[qn][0] += max(box[0], 1)
                return (qn, nissued[qn][0])
            sems[qn][1] += 16 * n_dma
            return (qn, sems[qn][1])

        def dma(e, out, in_, eng="sync"):
            sem = sems["SD" if eng == "sync" else "GD"]
            e.dma_start(out=out, in_=in_).then_inc(sem[0], 16)

        def barrier():
            tgt = sems["PH"][1] + 5
            for eng in ENGS:
                def _(e, tgt=tgt):
                    e.nop().then_inc(sems["PH"][0], 1)
                    e.wait_ge(sems["PH"][0], tgt)
                PROG[eng].append(_)
            sems["PH"][1] = tgt

        def quant16(src, dst, cols, r6, sc_, za, zb, deps):
            nb = cols // 16

            def v1(v):
                v.tensor_reduce(out=r6[:, 0:nb],
                                in_=src.rearrange("p (b k) -> p b k", k=16),
                                axis=AX, op=ALU.max, apply_absolute_value=True)
                v.tensor_scalar(out=r6[:, 0:nb], in0=r6[:, 0:nb],
                                scalar1=6e-12, scalar2=None, op0=ALU.max)
                v.tensor_scalar(out=sc_[:, 0:nb], in0=r6[:, 0:nb], scalar1=0.0,
                                scalar2=1.0 / 6.0, op0=ALU.add, op1=ALU.mult)
                v.reciprocal(out=r6[:, 0:nb], in_=r6[:, 0:nb])
                v.tensor_scalar(out=r6[:, 0:nb], in0=r6[:, 0:nb], scalar1=0.0,
                                scalar2=6.0, op0=ALU.add, op1=ALU.mult)
                v.tensor_tensor(
                    out=za.rearrange("p (b k) -> p b k", k=16),
                    in0=src.rearrange("p (b k) -> p b k", k=16),
                    in1=r6[:, 0:nb].unsqueeze(2).broadcast_to([128, nb, 16]),
                    op=ALU.mult)
                zi = za.bitcast(I32)
                bi = zb.bitcast(I32)
                v.tensor_scalar(out=bi, in0=zi, scalar1=s_ic[:, 3:4],
                                scalar2=None, op0=ALU.bitwise_and)
                v.tensor_scalar(out=dst, in0=zb,
                                scalar1=0.25, scalar2=1.0,
                                op0=ALU.is_gt, op1=ALU.mult)
                return v.tensor_scalar(out=zb, in0=zb,
                                       scalar1=0.5, scalar2=None, op0=ALU.max)
            tv = step("vector", deps, v1)

            def g1(g):
                bi = zb.bitcast(I32)
                return g.tensor_tensor(
                    out=bi, in0=bi,
                    in1=s_ic[:, 0:1].broadcast_to([128, cols]), op=ALU.add)
            tg = step("gpsimd", [tv], g1)

            def v2(v):
                zi = za.bitcast(I32)
                bi = zb.bitcast(I32)
                v.tensor_scalar(out=bi, in0=bi, scalar1=s_ic[:, 1:2],
                                scalar2=None, op0=ALU.bitwise_and)
                v.tensor_scalar(out=zi, in0=zi, scalar1=s_ic[:, 2:3],
                                scalar2=None, op0=ALU.bitwise_and)
                v.tensor_tensor(out=bi, in0=bi, in1=zi, op=ALU.bitwise_or)
                v.tensor_tensor(out=zb, in0=zb,
                                in1=dst, op=ALU.mult)
                return v.tensor_tensor(
                    out=dst.rearrange("p (b k) -> p b k", k=16),
                    in0=zb.rearrange("p (b k) -> p b k", k=16),
                    in1=sc_[:, 0:nb].unsqueeze(2).broadcast_to([128, nb, 16]),
                    op=ALU.mult)
            return step("vector", [tg], v2)

        _tp_last = [("SQ", 0)]

        def transp2(tiles, deps):
            evs = []
            tiles = tiles[:KTRN]
            if not tiles:
                return deps[0] if deps else ("VQ", 0)
            deps = list(deps) + [_tp_last[0]]
            for i, (src, dst, ef) in enumerate(tiles):
                bank = pt[i % 2]
                p, w = src.shape[0], src.shape[1]
                dd = list(deps) + ([evs[i - 2]] if i >= 2 else [])
                ttk = step("tensor", dd,
                           lambda t, src=src, w=w, p=p, bank=bank:
                           t.transpose(bank[0:w, 0:p], src, s_id[0:p, 0:p]))

                def sf(s, dst=dst, w=w, p=p, bank=bank, ef=ef):
                    if not KEVAC:
                        return None
                    if ef is None:
                        return s.copy(out=dst, in_=bank[0:w, 0:p])
                    return ef(s, bank[0:w, 0:p], dst)
                ev = step("scalar", [ttk], sf)
                evs.append(ev)
            _tp_last[0] = evs[-1]
            return evs[-1]

        def ln_stats(src_sb, dst_nmu, dst_rst, tmp, deps, dbg=0):
            def v1(v):
                for t in range(TT):
                    xt = src_sb[:, t * DM:(t + 1) * DM]
                    v.tensor_reduce(out=st_sum[:, t:t + 1], in_=xt, axis=AX,
                                    op=ALU.add)
                    v.tensor_tensor(out=tmp[:, 0:DM], in0=xt, in1=xt,
                                    op=ALU.mult)
                    v.tensor_reduce(out=st_ssq[:, t:t + 1], in_=tmp[:, 0:DM],
                                    axis=AX, op=ALU.add)
                v.tensor_scalar(out=dst_nmu[:], in0=st_sum[:], scalar1=0.0,
                                scalar2=-1.0 / DM, op0=ALU.add, op1=ALU.mult)
                v.tensor_scalar(out=st_ssq[:], in0=st_ssq[:], scalar1=0.0,
                                scalar2=1.0 / DM, op0=ALU.add, op1=ALU.mult)
                v.tensor_tensor(out=dst_rst[:], in0=dst_nmu[:], in1=dst_nmu[:],
                                op=ALU.mult)
                v.tensor_tensor(out=dst_rst[:], in0=st_ssq[:], in1=dst_rst[:],
                                op=ALU.subtract)
                return v.tensor_scalar(out=dst_rst[:], in0=dst_rst[:],
                                       scalar1=LNEPS, scalar2=None, op0=ALU.add)
            tv = step("vector", deps, v1)
            if dbg:
                tv = step("vector", [tv], lambda v: v.tensor_copy(
                    out=st_snap[:, 4 * TT:5 * TT], in_=dst_rst[:]))

            def s1(s):
                s.activation(out=dst_rst[:], in_=dst_rst[:], func=AF.Ln)
                return s.activation(out=dst_rst[:], in_=dst_rst[:],
                                    func=AF.Exp, scale=-0.5)
            ts1 = step("scalar", [tv], s1)
            if dbg:
                ts1 = step("vector", [ts1], lambda v: v.tensor_copy(
                    out=st_snap[:, 5 * TT:6 * TT], in_=dst_rst[:]))
            return ts1

        wa_stack = contextlib.ExitStack()
        WA = wa_stack.enter_context
        w_qa = WA(nc.sbuf_tensor("w_qa", [128, DKT * 512], BF16))
        w_ka = WA(nc.sbuf_tensor("w_ka", [128, DKT * 512], BF16))
        w_va = WA(nc.sbuf_tensor("w_va", [128, DKT * 512], BF16))
        w_aa = WA(nc.sbuf_tensor("w_aa", [128, DKT * 512], BF16))
        w_oa = WA(nc.sbuf_tensor("w_oa", [128, 4 * DM], BF16))
        s_gx = WA(nc.sbuf_tensor("s_gx", [128, DM], F32))
        s_bx = WA(nc.sbuf_tensor("s_bx", [128, DM], F32))

        with contextlib.ExitStack() as p0:
            P0 = p0.enter_context
            fw32 = P0(nc.sbuf_tensor("fw32", [128, 24 * 128], F32))
            fq16 = P0(nc.sbuf_tensor("fq16", [128, 24 * 128], BF16))
            fsh = P0(nc.sbuf_tensor("fsh", [128, 24 * 128], BF16))
            qza = P0(nc.sbuf_tensor("qza", [128, 24 * 128], F32))
            qzb = P0(nc.sbuf_tensor("qzb", [128, 24 * 128], F32))
            qr6 = P0(nc.sbuf_tensor("qr6", [128, 24 * 8], F32))
            qsc = P0(nc.sbuf_tensor("qsc", [128, 24 * 8], F32))

            def d0(sy):
                for dst, src in ((s_id, ident), (s_mtu, m_triu),
                                 (s_mt1, m_tril1), (s_i64, i64b),
                                 (s_scm, scanm), (s_ic, iconst), (s_pm, pm),
                                 (s_pmi, pmi), (s_baa, baa), (s_gx, gamx),
                                 (s_bx, betx), (s_on8, ones8)):
                    dma(sy, dst[:], src)
            t_con = step("sync", [], d0, n_dma=12)

            def d1(g):
                for t in range(TT):
                    dma(g, x_sb[:, t * DM:(t + 1) * DM],
                        x_in[128 * t:128 * (t + 1), :], "gpsimd")
                for dst, src in ((w_qa, wqa), (w_ka, wka), (w_va, wva),
                                 (w_aa, waa)):
                    for kt in range(DKT):
                        dma(g, dst[:, kt * 512:(kt + 1) * 512],
                            src[128 * kt:128 * (kt + 1), :], "gpsimd")
                for kt in range(4):
                    dma(g, w_oa[:, kt * DM:(kt + 1) * DM],
                        woa[128 * kt:128 * (kt + 1), :], "gpsimd")
            t_xw = step("gpsimd_dma", [], d1, n_dma=TT + 4 * DKT + 4)

            t_prev = t_con
            tout_prev = None
            ttr_prev = None
            offs = [(wg_sl, 0, ITL), (wu_sl, ICH, ITL), (wd_sl, 2 * ICH, 24)]
            for si, (src_h, dst_off, ntl) in enumerate(offs):
                cols = ntl * 128 if si < 2 else 3 * DKT * 128
                def ld(sy, src_h=src_h, cols=cols):
                    nrow = src_h.shape[0] // 128
                    ncol = src_h.shape[1] // 128
                    for s in range(ncol):
                        for kt in range(nrow):
                            j = s * nrow + kt
                            dma(sy, fw32[:, j * 128:(j + 1) * 128],
                                src_h[128 * kt:128 * (kt + 1),
                                      128 * s:128 * (s + 1)])
                tl = step("sync", [t_prev], ld, n_dma=cols // 128)
                tq = quant16(fw32[0:128, 0:cols], fq16[0:128, 0:cols], cols,
                             qr6, qsc, qza[0:128, 0:cols], qzb[0:128, 0:cols],
                             [tl, t_con] + ([ttr_prev] if ttr_prev else []))
                n_tr = cols // 128
                tiles = [(fq16[0:128, i * 128:(i + 1) * 128],
                          fsh[0:128, i * 128:(i + 1) * 128], None)
                         for i in range(n_tr)]
                ttr = transp2(tiles, [tq, t_con] +
                              ([tout_prev] if tout_prev else []))
                tout = step("gpsimd_dma", [ttr],
                            lambda g, dst_off=dst_off, cols=cols:
                            dma(g, ag0_i.ap()[:, dst_off:dst_off + cols],
                                fsh[:, 0:cols], "gpsimd"), n_dma=1)
                tout_prev = tout
                ttr_prev = ttr
                t_prev = tq
                if si == 0 and os.environ.get("KDW"):
                    def d0(g):
                        dma(g, dbg_out[:, 0:256], fw32[:, 2560:2816], "gpsimd")
                        dma(g, dbg_out[:, 256:512], fq16[:, 2560:2816],
                            "gpsimd")
                        dma(g, dbg_out[:, 512:768], fsh[:, 2560:2816],
                            "gpsimd")
                    t_prev = step("gpsimd_dma", [ttr], d0, n_dma=3)
            t_lastout = tout

            def dcc(g):
                if NOAG:
                    return None
                g.collective_compute(
                    "AllGather", ALU.bypass, replica_groups=[list(range(8))],
                    ins=[ag0_i.ap()], outs=[ag0_o.ap()],
                ).then_inc(sems["CC"][0], 1)
            step("gpsimd", [t_lastout, ("GD", sems["GD"][1])], dcc)
            if not NOAG:
                sems["CC"][1] += 1
            t_ag0 = ("CC", sems["CC"][1])

            barrier()

        DBG_APS = []
        if STAGE >= 1:
            with contextlib.ExitStack() as pa:
                P = pa.enter_context
                f32a = P(nc.sbuf_tensor("f32a", [128, 4 * DM], F32))
                f32b = P(nc.sbuf_tensor("f32b", [128, 4 * SCT], F32))
                rel_f = P(nc.sbuf_tensor("rel_f", [128, 4 * SCT], F32))
                exp_f = P(nc.sbuf_tensor("exp_f", [128, 4 * SCT], F32))
                zbf = P(nc.sbuf_tensor("zbf", [128, 4 * DM], BF16))
                zcm = P(nc.sbuf_tensor("zcm", [128, DKT * SCT], BF16))
                qcm = P(nc.sbuf_tensor("qcm", [128, 4 * SCT], BF16))
                kcm = P(nc.sbuf_tensor("kcm", [128, 4 * SCT], BF16))
                qt_ = P(nc.sbuf_tensor("qt_", [128, 4 * SCT], BF16))
                kt_ = P(nc.sbuf_tensor("kt_", [128, 4 * SCT], BF16))
                kh_ = P(nc.sbuf_tensor("kh_", [128, 4 * SCT], BF16))
                vtk = P(nc.sbuf_tensor("vtk", [128, 4 * 512], BF16))
                ktk = P(nc.sbuf_tensor("ktk", [128, 4 * 512], BF16))
                rnt = P(nc.sbuf_tensor("rnt", [128, 4 * 8], F32))
                scmb = P(nc.sbuf_tensor("scmb", [128, 8 * 64], BF16))
                khtb = P(nc.sbuf_tensor("khtb", [128, 8 * 64], BF16))
                s0g = P(nc.sbuf_tensor("s0g", [128, 4 * 64], BF16))
                pref = P(nc.sbuf_tensor("pref", [128, 4], F32))
                endb = P(nc.sbuf_tensor("endb", [128, 4 * 8], F32))
                o0a = P(nc.sbuf_tensor("o0a", [128, 4 * NT], BF16))
                qht = P(nc.sbuf_tensor("qht", [128, 4 * NT], BF16))
                agb = P(nc.sbuf_tensor("agb", [128, 4 * 64 + 4], BF16))
                sing = P(nc.sbuf_tensor("sing", [128, 4 * 64], BF16))
                aef = P(nc.sbuf_tensor("aef", [128, 4], F32))
                aef2 = P(nc.sbuf_tensor("aef2", [128, 64], F32))

                t_st = ln_stats(x_sb, st_nmu, st_rst, f32a, [t_xw], dbg=1)

                def v_st2(v):
                    v.tensor_scalar(out=st_nm2[:], in0=st_sum[:], scalar1=0.0,
                                    scalar2=1.0 / DM, op0=ALU.add, op1=ALU.mult)
                    v.tensor_tensor(out=st_nm2[:], in0=st_nm2[:], in1=st_nmu[:],
                                    op=ALU.add)
                    v.tensor_tensor(out=st_nm2[:], in0=st_nm2[:], in1=st_rst[:],
                                    op=ALU.mult)
                    v.tensor_tensor(out=st_t1[:], in0=st_nmu[:], in1=st_sum[:],
                                    op=ALU.mult)
                    v.tensor_scalar(out=st_t1[:], in0=st_t1[:], scalar1=0.0,
                                    scalar2=2.0 / DM, op0=ALU.add, op1=ALU.mult)
                    v.tensor_tensor(out=st_t2[:], in0=st_nmu[:], in1=st_nmu[:],
                                    op=ALU.mult)
                    v.tensor_tensor(out=st_t1[:], in0=st_t1[:], in1=st_t2[:],
                                    op=ALU.add)
                    v.tensor_tensor(out=st_t1[:], in0=st_t1[:], in1=st_ssq[:],
                                    op=ALU.add)
                    v.tensor_tensor(out=st_t2[:], in0=st_rst[:], in1=st_rst[:],
                                    op=ALU.mult)
                    v.tensor_tensor(out=st_t1[:], in0=st_t1[:], in1=st_t2[:],
                                    op=ALU.mult)
                    v.tensor_tensor(out=st_t2[:], in0=st_nm2[:], in1=st_nm2[:],
                                    op=ALU.mult)
                    v.tensor_tensor(out=st_rs2[:], in0=st_t1[:], in1=st_t2[:],
                                    op=ALU.subtract)
                    v.tensor_scalar(out=st_rs2[:], in0=st_rs2[:], scalar1=LNEPS,
                                    scalar2=None, op0=ALU.add)
                    return v.tensor_scalar(out=st_nm2[:], in0=st_nm2[:],
                                           scalar1=0.0, scalar2=-1.0,
                                           op0=ALU.add, op1=ALU.mult)
                tv2 = step("vector", [t_st], v_st2)

                t_st2 = step("scalar", [tv2], lambda s: (
                    s.activation(out=st_rs2[:], in_=st_rs2[:], func=AF.Ln),
                    s.activation(out=st_rs2[:], in_=st_rs2[:], func=AF.Exp,
                                 scale=-0.5))[-1])

                def v_snap(v):
                    v.tensor_copy(out=st_snap[:, 0:TT], in_=st_nmu[:])
                    v.tensor_copy(out=st_snap[:, TT:2 * TT], in_=st_rst[:])
                    v.tensor_copy(out=st_snap[:, 2 * TT:3 * TT], in_=st_nm2[:])
                    return v.tensor_copy(out=st_snap[:, 3 * TT:4 * TT],
                                         in_=st_rs2[:])
                t_snap = step("vector", [t_st2], v_snap)

                t_init = step("vector", [], lambda v: (
                    v.memset(s0g[:], 0.0), v.memset(pref[:], 0.0),
                    v.memset(o0a[:], 0.0), v.memset(qht[:], 0.0))[-1])

                bank_free = {i: ("VQ", 0) for i in range(6)}
                t_sc_ev = ("VQ", 0)
                t_o_ev = ("VQ", 0)
                t_kt_ev = ("VQ", 0)
                t_kt_sc_prev = ("SQ", 0)
                t_s_ev = t_init
                for u in range(SC if SUB >= 4 else (1 if SUB >= 2 else 0)):
                    def vz(v, u=u):
                        for i in range(4):
                            t = 4 * u + i
                            xt = x_sb[:, t * DM:(t + 1) * DM]
                            sl = slice(i * DM, (i + 1) * DM)
                            v.tensor_scalar(out=f32a[:, sl], in0=xt,
                                            scalar1=st_nmu[:, t:t + 1],
                                            scalar2=st_rst[:, t:t + 1],
                                            op0=ALU.add, op1=ALU.mult)
                            v.tensor_scalar(out=f32a[:, sl], in0=f32a[:, sl],
                                            scalar1=st_nm2[:, t:t + 1],
                                            scalar2=st_rs2[:, t:t + 1],
                                            op0=ALU.add, op1=ALU.mult)
                            v.tensor_tensor(out=f32a[:, sl], in0=f32a[:, sl],
                                            in1=s_gx[:], op=ALU.mult)
                            v.tensor_tensor(out=zbf[:, sl], in0=f32a[:, sl],
                                            in1=s_bx[:], op=ALU.add)
                        return None
                    t_z = step("vector", [t_st2, t_con], vz)

                    if SUBB < 2:
                        continue
                    zsrc = x_sb if KSRC else zbf
                    ztiles = [(zsrc[0:128, i * DM + kt * 128:
                                    i * DM + (kt + 1) * 128],
                               zcm[0:128, kt * SCT + i * 128:
                                   kt * SCT + (i + 1) * 128],
                               None) for i in range(4) for kt in range(DKT)]
                    t_zcm = transp2(ztiles, [t_z])

                    if SUBB < 3:
                        continue
                    pev = {}
                    for bank, w_sb, dst, bias in (
                            (0, w_qa, qcm, None), (1, w_ka, kcm, None),
                            (2, w_aa, None, s_baa))[:KPROJ]:
                        for mt in range(4):
                            def tp(t, bank=bank, w_sb=w_sb, mt=mt):
                                for kt in range(DKT):
                                    t.matmul(pb[bank][0:128, 0:512],
                                             w_sb[:, kt * 512 + mt * 128:
                                                  kt * 512 + (mt + 1) * 128],
                                             zcm[:, kt * SCT:(kt + 1) * SCT],
                                             start=(kt == 0), stop=(kt == DKT - 1))
                            tk = step("tensor", [t_zcm, bank_free[bank], t_xw], tp)

                            def vp(s, bank=bank, dst=dst, mt=mt, bias=bias):
                                if dst is not None:
                                    return s.copy(
                                        out=dst[:, mt * SCT:(mt + 1) * SCT],
                                        in_=pb[bank][0:128, 0:512])
                                return s.add(
                                    out=f32b[:, mt * SCT:(mt + 1) * SCT],
                                    in_=pb[bank][0:128, 0:512],
                                    add=s_baa[:, mt:mt + 1])
                            ev = step("scalar", [tk], vp)
                            bank_free[bank] = ev
                            pev[(bank, mt)] = ev
                    for i in range(4 if KPROJ >= 4 else 0):
                        def tpv(t, i=i):
                            for kt in range(DKT):
                                t.matmul(pb[3][0:128, 0:512],
                                         zcm[:, kt * SCT + i * 128:
                                             kt * SCT + (i + 1) * 128],
                                         w_va[:, kt * 512:(kt + 1) * 512],
                                         start=(kt == 0), stop=(kt == DKT - 1))
                        tk = step("tensor", [t_zcm, bank_free[3], t_xw], tpv)
                        ev = step("scalar", [tk], lambda s, i=i: s.copy(
                            out=vtk[:, i * 512:(i + 1) * 512],
                            in_=pb[3][0:128, 0:512]))
                        bank_free[3] = ev
                        pev[(3, i)] = ev

                    if SUBB < 4:
                        continue
                    ktiles = [(kcm[0:128, g * SCT + i * 128:g * SCT + (i + 1) * 128],
                               ktk[0:128, i * 512 + g * 128:i * 512 + (g + 1) * 128],
                               None) for i in range(4) for g in range(4)]
                    t_ktk = transp2(ktiles, [pev[(1, 0)], pev[(1, 1)],
                                             pev[(1, 2)], pev[(1, 3)]])

                    def v_rn(v, u=u):
                        for i in range(4):
                            v.tensor_tensor(out=f32a[:, 0:512],
                                            in0=ktk[:, i * 512:(i + 1) * 512],
                                            in1=ktk[:, i * 512:(i + 1) * 512],
                                            op=ALU.mult)
                            v.tensor_reduce(
                                out=rnt[:, i * 8:(i + 1) * 8],
                                in_=f32a[:, 0:512].rearrange(
                                    "p (h d) -> p h d", d=64),
                                axis=AX, op=ALU.add)
                        return v.tensor_scalar(out=rnt[:], in0=rnt[:],
                                               scalar1=1e-24, scalar2=None,
                                               op0=ALU.max)
                    t_k2 = step("vector", [t_ktk], v_rn)
                    t_rn = step("scalar", [t_k2], lambda s: (
                        s.activation(out=rnt[:], in_=rnt[:], func=AF.Ln),
                        s.activation(out=rnt[:], in_=rnt[:], func=AF.Exp,
                                     scale=-0.5))[-1])

                    if SUBB < 5:
                        continue
                    def s_la(s):
                        s.activation(out=f32a[:, 0:4 * SCT], in_=f32b[:],
                                     func=AF.Abs)
                        s.activation(out=f32a[:, 0:4 * SCT],
                                     in_=f32a[:, 0:4 * SCT], func=AF.Exp,
                                     scale=-1.0)
                        return s.activation(out=f32a[:, 0:4 * SCT],
                                            in_=f32a[:, 0:4 * SCT], func=AF.Ln,
                                            bias=1.0)
                    tsl = step("scalar", [pev[(2, m)] for m in range(4)], s_la)

                    def v_la(v, u=u):
                        zc = s_ic[:, 5:6].bitcast(F32).broadcast_to([128, 4 * SCT])
                        v.tensor_tensor(out=f32b[:], in0=f32b[:], in1=zc,
                                        op=ALU.min)
                        v.tensor_tensor(out=f32b[:], in0=f32b[:],
                                        in1=f32a[:, 0:4 * SCT], op=ALU.subtract)
                        for g in range(4):
                            v.tensor_tensor_scan(
                                out=rel_f[:, g * SCT:(g + 1) * SCT],
                                data0=s_scm[:],
                                data1=f32b[:, g * SCT:(g + 1) * SCT],
                                initial=0.0, op0=ALU.mult, op1=ALU.add)
                        return v.tensor_copy(
                            out=endb[:].rearrange("p (g c) -> p g c", c=8),
                            in_=rel_f[:].rearrange("p (g ct) -> p g ct", g=4)
                            .rearrange("p g (c t) -> p g c t", t=CH)[:, :, :,
                                                                     CH - 1])
                    tvla = step("vector", [tsl], v_la)

                    def v_pr(v, u=u):
                        for g in range(4):
                            v.tensor_tensor_scan(
                                out=f32a[:, g * 8:(g + 1) * 8], data0=s_on8[:],
                                data1=endb[:, g * 8:(g + 1) * 8], initial=0.0,
                                op0=ALU.mult, op1=ALU.add)
                        v.tensor_tensor(out=f32a[:, 0:32], in0=f32a[:, 0:32],
                                        in1=endb[:], op=ALU.subtract)
                        for g in range(4):
                            v.tensor_scalar(out=f32a[:, g * 8:(g + 1) * 8],
                                            in0=f32a[:, g * 8:(g + 1) * 8],
                                            scalar1=pref[:, g:g + 1],
                                            scalar2=None, op0=ALU.add)
                        for g in range(4):
                            v.tensor_reduce(out=aef[:, g:g + 1],
                                            in_=endb[:, g * 8:(g + 1) * 8],
                                            axis=AX, op=ALU.add)
                        v.tensor_tensor(out=pref[:], in0=pref[:], in1=aef[:],
                                        op=ALU.add)
                        return v.tensor_tensor(
                            out=f32b[:].rearrange("p (gc t) -> p gc t", t=CH),
                            in0=rel_f[:].rearrange("p (gc t) -> p gc t", t=CH),
                            in1=f32a[:, 0:32].unsqueeze(2).broadcast_to(
                                [128, 32, CH]),
                            op=ALU.add)
                    tvpr = step("vector", [tvla], v_pr)

                    if SUBB < 6:
                        continue
                    t_e1 = step("scalar", [tvpr], lambda s: s.activation(
                        out=exp_f[:], in_=rel_f[:], func=AF.Exp))
                    t_q1 = step("vector", [t_e1, pev[(0, 0)], pev[(0, 1)],
                                           pev[(0, 2)], pev[(0, 3)]],
                                lambda v: v.tensor_tensor(out=qt_[:], in0=qcm[:],
                                                          in1=exp_f[:],
                                                          op=ALU.mult))
                    t_ae = step("vector", [t_e1], lambda v: v.tensor_copy(
                        out=endb[:].rearrange("p (g c) -> p g c", c=8),
                        in_=exp_f[:].rearrange("p (g ct) -> p g ct", g=4)
                        .rearrange("p g (c t) -> p g c t", t=CH)[:, :, :,
                                                                 CH - 1]))
                    t_e2 = step("scalar", [t_q1, t_ae], lambda s: s.activation(
                        out=exp_f[:], in_=rel_f[:], func=AF.Exp, scale=-1.0))
                    t_k1 = step("vector", [t_e2], lambda v: v.tensor_tensor(
                        out=kt_[:], in0=kcm[:], in1=exp_f[:], op=ALU.mult))
                    def v_e3in(v):
                        return v.tensor_tensor(
                            out=exp_f[:].rearrange("p (gc t) -> p gc t", t=CH),
                            in0=rel_f[:].rearrange(
                                "p (gc t) -> p gc t", t=CH)[:, :, CH - 1:CH]
                            .broadcast_to([128, 32, CH]),
                            in1=rel_f[:].rearrange("p (gc t) -> p gc t", t=CH),
                            op=ALU.subtract)
                    t_e3i = step("vector", [t_k1], v_e3in)
                    t_e3 = step("scalar", [t_e3i], lambda s: s.activation(
                        out=exp_f[:], in_=exp_f[:], func=AF.Exp))
                    t_k2b = step("vector", [t_e3], lambda v: v.tensor_tensor(
                        out=kh_[:], in0=kcm[:], in1=exp_f[:], op=ALU.mult))
                    t_e4 = step("scalar", [t_k2b], lambda s: s.activation(
                        out=f32b[:], in_=f32b[:], func=AF.Exp))
                    t_qh = step("vector", [t_e4], lambda v, u=u: v.tensor_tensor(
                        out=qht[:].rearrange("p (g n) -> p g n", n=NT)[
                            :, :, u * SCT:(u + 1) * SCT],
                        in0=qcm[:].rearrange("p (g n) -> p g n", n=SCT),
                        in1=f32b[:].rearrange("p (g n) -> p g n", n=SCT),
                        op=ALU.mult))

                    for c in range(8 if SUB >= 3 else 0):
                        tc, off = c // 2, 64 * (c % 2)
                        cs = slice(c * CH, (c + 1) * CH)

                        def t_sc(t, u=u, c=c, off=off):
                            for h in range(8):
                                g, ho = h // 2, 64 * (h % 2)
                                t.matmul(pb[0][off:off + 64, 64 * h:64 * h + 64],
                                         kt_[ho:ho + 64, g * SCT + c * CH:
                                             g * SCT + (c + 1) * CH],
                                         qt_[ho:ho + 64, g * SCT + c * CH:
                                             g * SCT + (c + 1) * CH],
                                         start=True, stop=True)
                        tk_sc = step("tensor", [t_q1, t_k1, t_sc_ev], t_sc)

                        def v_sc(v, c=c, tc=tc, off=off):
                            v.tensor_tensor(
                                out=scmb[off:off + 64, :].rearrange(
                                    "p (h t) -> p h t", t=64),
                                in0=pb[0][off:off + 64, :].rearrange(
                                    "p (h t) -> p h t", t=64),
                                in1=rnt[off:off + 64, tc * 8:(tc + 1) * 8]
                                .unsqueeze(2).broadcast_to([64, 8, 64]),
                                op=ALU.mult)
                            return v.tensor_tensor(
                                out=scmb[off:off + 64, :].rearrange(
                                    "p (h t) -> p h t", t=64),
                                in0=scmb[off:off + 64, :].rearrange(
                                    "p (h t) -> p h t", t=64),
                                in1=s_mtu[off:off + 64, :].unsqueeze(1)
                                .broadcast_to([64, 8, 64]),
                                op=ALU.mult)
                        t_sc_ev = step("vector", [tk_sc, t_rn], v_sc)

                        def t_o(t, u=u, c=c, tc=tc, off=off):
                            for h in range(8):
                                g, ho = h // 2, 64 * (h % 2)
                                t.matmul(pb[1][ho:ho + 64, g * 64:(g + 1) * 64],
                                         vtk[off:off + 64, tc * 512 + 64 * h:
                                             tc * 512 + 64 * h + 64],
                                         scmb[off:off + 64, 64 * h:64 * h + 64],
                                         start=True, stop=False)
                                t.matmul(pb[1][ho:ho + 64, g * 64:(g + 1) * 64],
                                         s0g[ho:ho + 64, g * 64:(g + 1) * 64],
                                         qt_[ho:ho + 64, g * SCT + c * CH:
                                             g * SCT + (c + 1) * CH],
                                         start=False, stop=True)
                        tk_o = step("tensor", [t_sc_ev, t_s_ev, t_o_ev,
                                               pev[(3, tc)]], t_o)

                        def v_o(v, u=u, c=c):
                            return v.tensor_copy(
                                out=o0a[:].rearrange("p (g n) -> p g n", n=NT)[
                                    :, :, u * SCT + c * CH:u * SCT + (c + 1) * CH],
                                in_=pb[1][0:128, 0:256].rearrange(
                                    "p (g t) -> p g t", t=64))
                        t_o_ev = step("vector", [tk_o], v_o)

                        def t_kt(t, c=c, off=off):
                            for h in range(8):
                                g, ho = h // 2, 64 * (h % 2)
                                t.transpose(pt[0][off:off + 64, 64 * h:64 * h + 64],
                                            kh_[ho:ho + 64, g * SCT + c * CH:
                                                g * SCT + (c + 1) * CH],
                                            s_i64[ho:ho + 64, :])
                        tk_kt = step("tensor", [t_k2b, t_o_ev, t_kt_sc_prev],
                                     t_kt)

                        def s_kt(s, c=c, off=off):
                            return s.copy(out=khtb[off:off + 64, :],
                                          in_=pt[0][off:off + 64, 0:512])
                        t_kt_sc = step("scalar", [tk_kt], s_kt)

                        def v_kt(v, c=c, tc=tc, off=off):
                            return v.tensor_tensor(
                                out=khtb[off:off + 64, :].rearrange(
                                    "p (h j) -> p h j", j=64),
                                in0=khtb[off:off + 64, :].rearrange(
                                    "p (h j) -> p h j", j=64),
                                in1=rnt[off:off + 64, tc * 8:(tc + 1) * 8]
                                .unsqueeze(2).broadcast_to([64, 8, 64]),
                                op=ALU.mult)
                        t_kt_ev = step("vector", [t_kt_sc], v_kt)
                        t_kt_sc_prev = t_kt_sc

                        def t_su(t, c=c, tc=tc, off=off):
                            for h in range(8):
                                g, ho = h // 2, 64 * (h % 2)
                                t.matmul(pb[2][ho:ho + 64, g * 64:(g + 1) * 64],
                                         khtb[off:off + 64, 64 * h:64 * h + 64],
                                         vtk[off:off + 64, tc * 512 + 64 * h:
                                             tc * 512 + 64 * h + 64],
                                         start=True, stop=True)
                        tk_su = step("tensor", [t_kt_ev, t_s_ev], t_su)

                        def v_su(v, u=u, c=c):
                            last = None
                            for h in range(8):
                                g, ho = h // 2, 64 * (h % 2)
                                last = v.scalar_tensor_tensor(
                                    out=s0g[ho:ho + 64, g * 64:(g + 1) * 64],
                                    in0=s0g[ho:ho + 64, g * 64:(g + 1) * 64],
                                    scalar=endb[ho:ho + 64, g * 8 + c:g * 8 + c + 1],
                                    in1=pb[2][ho:ho + 64, g * 64:(g + 1) * 64],
                                    op0=ALU.mult, op1=ALU.add)
                            return last
                        t_s_ev = step("vector", [tk_su, t_ae], v_su)

                if SUB >= 5:
                    def v_agb(v):
                        return v.tensor_copy(out=agb[:, 0:256], in_=s0g[:])
                    t_ab = step("vector", [t_s_ev], v_agb)
                    t_ap = step("scalar", [t_ab], lambda s: s.activation(
                        out=agb[:, 256:260], in_=pref[:], func=AF.Exp))
                    t_agd = step("gpsimd_dma", [t_ap], lambda g: dma(
                        g, ag1_i.ap()[:], agb[:], "gpsimd"), n_dma=1)

                    def dcc1(g):
                        g.collective_compute(
                            "AllGather", ALU.bypass, replica_groups=[list(range(8))],
                            ins=[ag1_i.ap()], outs=[ag1_o.ap()],
                        ).then_inc(sems["CC"][0], 1)
                    step("gpsimd", [t_agd, ("GD", sems["GD"][1])], dcc1)
                    sems["CC"][1] += 1
                    t_ag1 = ("CC", sems["CC"][1])

                    t_z0 = step("vector", [t_ag1], lambda v: v.memset(sing[:], 0.0))
                    t_fold = t_z0
                    for j in range(8):
                        t_lj = step("gpsimd_dma", [t_fold], lambda g, j=j: dma(
                            g, agb[:], ag1_o.ap()[128 * j:128 * (j + 1), :],
                            "gpsimd"), n_dma=1)

                        def v_fold(v, j=j):
                            v.tensor_scalar(out=aef[:], in0=agb[:, 256:260],
                                            scalar1=0.0, scalar2=s_pm[:, j:j + 1],
                                            op0=ALU.add, op1=ALU.mult)
                            v.tensor_tensor(out=aef[:], in0=aef[:],
                                            in1=s_pmi[:, j:j + 1].broadcast_to(
                                                [128, 4]), op=ALU.add)
                            last = None
                            for g in range(4):
                                v.tensor_scalar(out=sing[:, g * 64:(g + 1) * 64],
                                                in0=sing[:, g * 64:(g + 1) * 64],
                                                scalar1=0.0,
                                                scalar2=aef[:, g:g + 1],
                                                op0=ALU.add, op1=ALU.mult)
                                v.tensor_scalar(
                                    out=aef2[:, 0:64],
                                    in0=agb[:, g * 64:(g + 1) * 64],
                                    scalar1=0.0, scalar2=s_pm[:, j:j + 1],
                                    op0=ALU.add, op1=ALU.mult)
                                last = v.tensor_tensor(
                                    out=sing[:, g * 64:(g + 1) * 64],
                                    in0=sing[:, g * 64:(g + 1) * 64],
                                    in1=aef2[:, 0:64], op=ALU.add)
                            return last
                        t_fold = step("vector", [t_lj], v_fold)

            DBG_APS.extend([(st_sum[:], 16), (st_ssq[:], 16),
                            (st_rst[:], 16), (st_rs2[:], 16),
                            (st_snap[:], 96)])
            if SUB < 5:
                t_fold = t_init
            if SUB >= 6:
                t_oc_pp = {0: t_fold, 1: t_fold}
                t_oc_ev = t_fold
                for idx in range(SC * 4):
                    u, gp = idx // 4, idx % 4
                    pr = idx % 2
                    obk = 4 + pr

                    def t_ocr(t, u=u, gp=gp, obk=obk):
                        for half in range(2):
                            ho = 64 * half
                            t.matmul(pb[obk][ho:ho + 64, 0:512],
                                     sing[ho:ho + 64,
                                          gp * 64:(gp + 1) * 64],
                                     qht[ho:ho + 64, gp * NT + u * SCT:
                                         gp * NT + (u + 1) * SCT],
                                     start=True, stop=True)
                    tk = step("tensor", [t_oc_pp[pr], t_fold], t_ocr)

                    def v_ocr(v, u=u, gp=gp, obk=obk):
                        return v.scalar_tensor_tensor(
                            out=o0a[:, gp * NT + u * SCT:
                                    gp * NT + (u + 1) * SCT],
                            in0=pb[obk][0:128, 0:512], scalar=1.0,
                            in1=o0a[:, gp * NT + u * SCT:
                                    gp * NT + (u + 1) * SCT],
                            op0=ALU.mult, op1=ALU.add)
                    t_oc_ev = step("vector", [tk], v_ocr)
                    t_oc_pp[pr] = t_oc_ev
            else:
                t_oc_ev = t_fold

            if SUB >= 7:
                t_xpp = {0: t_oc_ev, 1: t_oc_ev}
                t_xev = t_oc_ev
                for tt in range(TT):
                    pr = tt % 2
                    xb0 = 2 * pr

                    def t_op(t, tt=tt, xb0=xb0):
                        for g in range(4):
                            for n in range(2):
                                t.matmul(pb[xb0 + n][0:128, 0:512],
                                         o0a[:, g * NT + tt * 128:
                                             g * NT + (tt + 1) * 128],
                                         w_oa[:, g * DM + n * 512:
                                              g * DM + (n + 1) * 512],
                                         start=(g == 0), stop=(g == 3))
                    tk = step("tensor", [t_xpp[pr], t_oc_ev], t_op)

                    def v_xr(v, tt=tt, xb0=xb0, pr=pr):
                        xt = x_sb[:, tt * DM:(tt + 1) * DM]
                        fbuf = f32a if pr == 0 else f32b
                        v.tensor_scalar(out=fbuf[:, 0:DM], in0=xt,
                                        scalar1=st_nmu[:, tt:tt + 1],
                                        scalar2=st_rst[:, tt:tt + 1],
                                        op0=ALU.add, op1=ALU.mult)
                        last = None
                        for n in range(2):
                            v.tensor_tensor(
                                out=fbuf[:, n * 512:(n + 1) * 512],
                                in0=fbuf[:, n * 512:(n + 1) * 512],
                                in1=pb[xb0 + n][0:128, 0:512], op=ALU.add)
                            last = v.tensor_tensor(
                                out=xt[0:128, n * 512:(n + 1) * 512],
                                in0=fbuf[:, n * 512:(n + 1) * 512],
                                in1=xt[0:128, n * 512:(n + 1) * 512],
                                op=ALU.add)
                        return last
                    t_xev = step("vector", [tk], v_xr)
                    t_xpp[pr] = t_xev

            barrier()

        def emit_epilogue(dbg=()):
            with contextlib.ExitStack() as pe:
                stg = pe.enter_context(
                    nc.sbuf_tensor("stg", [128, 2 * DM], F32))
                dbuf = pe.enter_context(
                    nc.sbuf_tensor("dbuf", [128, 2048], F32))
                def vdbg(v):
                    if os.environ.get("KDW"):
                        return v.memset(dbuf[0:1, 0:1], 0.0)
                    v.memset(dbuf[:], 0.0)
                    col = 0
                    for ap_, w in dbg:
                        v.tensor_copy(out=dbuf[:, col:col + w], in_=ap_)
                        col += w
                    return None
                tvd = step("vector", [("VQ", nissued["VQ"][0]),
                                      ("SQ", nissued["SQ"][0]),
                                      ("GQ", nissued["GQ"][0]),
                                      ("GD", sems["GD"][1])], vdbg)
                if not (os.environ.get("KDW") or os.environ.get("KDW2")):
                    step("sync", [tvd], lambda sy: dma(sy, dbg_out, dbuf[:]),
                         n_dma=1)
                tprev = ("VQ", nissued["VQ"][0])
                td = ("SD", sems["SD"][1])
                for t in range(TT):
                    tv = step("vector", [tprev, td], lambda v, t=t:
                              v.tensor_copy(
                                  out=stg[:, (t % 2) * DM:(t % 2 + 1) * DM],
                                  in_=x_sb[:, t * DM:(t + 1) * DM]))
                    td = step("sync", [tv], lambda sy, t=t: dma(
                        sy, x_out.rearrange("(t p) d -> p t d", p=128)[
                            :, t, :],
                        stg[:, (t % 2) * DM:(t % 2 + 1) * DM]), n_dma=1)
                    tprev = tv
                step("sync", [td], lambda sy: None)
                step("gpsimd", [("CC", sems["CC"][1])], lambda g: None)

        if STAGE == 1:
            emit_epilogue(dbg=DBG_APS)
            flush_prog()
            return nc
        wa_stack.close()

        ICCH = [(0, 512), (512, 512), (1024, 512), (1536, 512),
                (2048, 512), (2560, 256)]

        def emit_ffn(tag=""):
            with contextlib.ExitStack() as pf:
                P = pf.enter_context
                w_g = P(nc.sbuf_tensor("w_g" + tag, [128, DKT * ICH], BF16))
                w_u = P(nc.sbuf_tensor("w_u" + tag, [128, DKT * ICH], BF16))
                w_d = P(nc.sbuf_tensor("w_d" + tag, [128, ITL * DM], BF16))
                fb1 = P(nc.sbuf_tensor("fb1" + tag, [128, DM], F32))
                fzb = P(nc.sbuf_tensor("fzb" + tag, [128, 1408], F32))
                fr6 = P(nc.sbuf_tensor("fr6" + tag, [128, 64], F32))
                fsc = P(nc.sbuf_tensor("fsc" + tag, [128, 64], F32))
                fxq = P(nc.sbuf_tensor("fxq" + tag, [128, DM], BF16))
                fxqc = P(nc.sbuf_tensor("fxqc" + tag, [128, DKT * 128], BF16))
                fsg = P(nc.sbuf_tensor("fsg" + tag, [128, 512], F32))
                fha = P(nc.sbuf_tensor("fha" + tag, [128, ICH], F32))
                fhr6 = P(nc.sbuf_tensor("fhr6" + tag, [128, 88], F32))
                fhsc = P(nc.sbuf_tensor("fhsc" + tag, [128, 88], F32))
                fhq = P(nc.sbuf_tensor("fhq" + tag, [128, ICH], BF16))
                fhqc = P(nc.sbuf_tensor("fhqc" + tag, [128, 11 * 128],
                                        BF16))
                fdbg = None
                if os.environ.get("KDW2"):
                    fdbg = P(nc.sbuf_tensor("fdbg" + tag, [128, 1024], F32))

                def ldwg(sy):
                    for r in range(DKT):
                        slab = ag0_o.ap()[128 * r:128 * (r + 1), :]
                        dma(sy, w_g[:, r * ICH:(r + 1) * ICH], slab[:, 0:ICH])
                t_wg = step("sync", [t_ag0], ldwg, n_dma=DKT)

                def ldwu(sy):
                    for r in range(DKT):
                        slab = ag0_o.ap()[128 * r:128 * (r + 1), :]
                        dma(sy, w_u[:, r * ICH:(r + 1) * ICH],
                            slab[:, ICH:2 * ICH])
                t_wu = step("sync", [t_wg], ldwu, n_dma=DKT)

                def ldwd(sy):
                    for t in range(ITL):
                        rank, slot = t % 8, t // 8
                        slab = ag0_o.ap()[128 * rank:128 * (rank + 1), :]
                        dma(sy, w_d[:, t * DM:(t + 1) * DM],
                            slab[:, 2 * ICH + slot * DM:
                                 2 * ICH + (slot + 1) * DM])
                t_w = step("sync", [t_wu], ldwd, n_dma=ITL)
                if os.environ.get("KDW"):
                    def dwdbg(g):
                        dma(g, dbg_out[:, 768:1024], w_g[:, 2560:2816],
                            "gpsimd")
                        dma(g, dbg_out[:, 1024:1280],
                            ag0_o.ap()[0:128, 2560:2816], "gpsimd")
                        dma(g, dbg_out[:, 1280:1536], w_u[:, 2560:2816],
                            "gpsimd")
                        dma(g, dbg_out[:, 1536:1792],
                            w_d[:, 21 * DM + 768:21 * DM + 1024], "gpsimd")
                        dma(g, dbg_out[:, 1792:2048],
                            w_g[:, 7 * ICH + 2560:7 * ICH + 2816], "gpsimd")
                    step("gpsimd_dma", [t_w], dwdbg, n_dma=5)

                t_st = ln_stats(x_sb, st_nmu, st_rst, fb1,
                                [("VQ", nissued["VQ"][0]),
                                 ("SQ", nissued["SQ"][0])])
                t_res_prev = t_st
                bank_gu = {0: ("VQ", 0), 1: ("VQ", 0)}
                bank_d = ("VQ", 0)
                t_xc_prev = ("SQ", 0)
                t_vh_prev = ("VQ", 0)
                t_hc_prev = ("SQ", 0)
                for tt in range(TT):
                    def vx(v, tt=tt):
                        return v.tensor_scalar(
                            out=fb1[:], in0=x_sb[:, tt * DM:(tt + 1) * DM],
                            scalar1=st_nmu[:, tt:tt + 1],
                            scalar2=st_rst[:, tt:tt + 1],
                            op0=ALU.add, op1=ALU.mult)
                    t_x = step("vector", [t_st], vx)
                    t_q = quant16(fb1[0:128, 0:DM], fxq[0:128, 0:DM], DM,
                                  fr6, fsc, fb1[0:128, 0:DM],
                                  fzb[0:128, 0:DM], [t_x, t_xc_prev])
                    xt_tiles = [(fxq[0:128, k * 128:(k + 1) * 128],
                                 fxqc[0:128, k * 128:(k + 1) * 128], None)
                                for k in range(DKT)]
                    t_xc = transp2(xt_tiles, [t_q])
                    t_xc_prev = t_xc

                    t_h_ch = {}
                    for j, (ic0, icw) in enumerate(ICCH):
                        par = j % 2
                        gb0, gb1 = (0, 1) if par == 0 else (2, 3)

                        def tgu(t, ic0=ic0, icw=icw, gb0=gb0, gb1=gb1):
                            for kt in range(DKT):
                                t.matmul(pb[gb0][0:128, 0:icw],
                                         fxqc[:, kt * 128:(kt + 1) * 128],
                                         w_g[:, kt * ICH + ic0:
                                             kt * ICH + ic0 + icw],
                                         start=(kt == 0), stop=(kt == DKT - 1))
                            for kt in range(DKT):
                                t.matmul(pb[gb1][0:128, 0:icw],
                                         fxqc[:, kt * 128:(kt + 1) * 128],
                                         w_u[:, kt * ICH + ic0:
                                             kt * ICH + ic0 + icw],
                                         start=(kt == 0), stop=(kt == DKT - 1))
                        tk = step("tensor", [t_xc, bank_gu[par], t_wu], tgu)

                        def ssi(s, icw=icw, gb0=gb0):
                            return s.activation(out=fsg[:, 0:icw],
                                                in_=pb[gb0][0:128, 0:icw],
                                                func=AF.Silu)
                        t_si = step("scalar", [tk, t_vh_prev], ssi)

                        def vh(v, icw=icw, gb1=gb1, ic0=ic0):
                            return v.tensor_tensor(
                                out=fha[:, ic0:ic0 + icw],
                                in0=fsg[:, 0:icw],
                                in1=pb[gb1][0:128, 0:icw],
                                op=ALU.mult)
                        t_h = step("vector", [t_si], vh)
                        bank_gu[par] = t_h
                        t_vh_prev = t_h
                        t_h_ch[j] = t_h

                    t_dn_last = None
                    for p in range(2):
                        hs = slice(p * 1408, (p + 1) * 1408)
                        t_hq = quant16(fha[0:128, hs], fhq[0:128, hs],
                                       1408, fhr6, fhsc, fha[0:128, hs],
                                       fzb[0:128, 0:1408],
                                       [t_h_ch[2 + 3 * p], t_hc_prev])
                        hq_tiles = [(fhq[0:128, (11 * p + k) * 128:
                                         (11 * p + k + 1) * 128],
                                     fhqc[0:128, k * 128:(k + 1) * 128],
                                     None) for k in range(11)]
                        t_hc = transp2(hq_tiles, [t_hq])
                        t_hc_prev = t_hc

                        def tdn(t, p=p):
                            for kt4 in range(11):
                                gt = 11 * p + kt4
                                for n in range(2):
                                    t.matmul(
                                        pb[4 + n][0:128, 0:512],
                                        fhqc[:, kt4 * 128:(kt4 + 1) * 128],
                                        w_d[:, gt * DM + n * 512:
                                            gt * DM + (n + 1) * 512],
                                        start=(p == 0 and kt4 == 0),
                                        stop=(p == 1 and kt4 == 10))
                        t_dn_last = step("tensor", [t_hc, bank_d, t_w], tdn)

                    t_dump = None
                    if tt == 0 and os.environ.get("KDW2"):
                        def dtile(g):
                            dma(g, dbg_out[:, 0:1024], fdbg[:], "gpsimd")
                            dma(g, dbg_out[:, 1024:1536], fsg[:], "gpsimd")
                            dma(g, dbg_out[:, 1536:2048], fh[:], "gpsimd")
                        t_dump = step("gpsimd_dma",
                                      [t_dn_last,
                                       ("VQ", nissued["VQ"][0]),
                                       ("SQ", nissued["SQ"][0])],
                                      dtile, n_dma=3)

                    def vres(v, tt=tt):
                        xt = x_sb[:, tt * DM:(tt + 1) * DM]
                        last = None
                        for n in range(2):
                            sl = slice(n * 512, (n + 1) * 512)
                            last = v.scalar_tensor_tensor(
                                out=xt[0:128, sl], in0=xt[0:128, sl],
                                scalar=1.0, in1=pb[4 + n][0:128, 0:512],
                                op0=ALU.mult, op1=ALU.add)
                        return last
                    t_res_prev = step("vector",
                                      [t_dn_last] +
                                      ([t_dump] if t_dump else []), vres)
                    bank_d = t_res_prev

            barrier()
            return t_res_prev

        def emit_attn_b():
            with contextlib.ExitStack() as pbs:
                P = pbs.enter_context
                w_qb = P(nc.sbuf_tensor("w_qb", [128, DKT * DM], BF16))
                w_kb = P(nc.sbuf_tensor("w_kb", [128, DKT * DM], BF16))
                w_vb = P(nc.sbuf_tensor("w_vb", [128, DKT * DM], BF16))
                w_bb = P(nc.sbuf_tensor("w_bb", [128, DKT * 8], BF16))
                s_lw = P(nc.sbuf_tensor("s_lw", [128, DM], F32))
                s_lb = P(nc.sbuf_tensor("s_lb", [128, DM], F32))
                s_mu = P(nc.sbuf_tensor("s_mu", [128, 128], F32))
                s_mls = P(nc.sbuf_tensor("s_mls", [128, 128], F32))
                zbfb = P(nc.sbuf_tensor("zbfb", [128, 4 * DM], BF16))
                zcmb = P(nc.sbuf_tensor("zcmb", [128, DKT * SCT], BF16))
                q_cm = P(nc.sbuf_tensor("q_cm", [128, 8 * SCT], BF16))
                k_cm = P(nc.sbuf_tensor("k_cm", [128, 8 * SCT], BF16))
                khcm = P(nc.sbuf_tensor("khcm", [128, 8 * SCT], BF16))
                v_tk = P(nc.sbuf_tensor("v_tk", [128, 4 * DM], BF16))
                k_tk = P(nc.sbuf_tensor("k_tk", [128, 4 * DM], BF16))
                bb_t = P(nc.sbuf_tensor("bb_t", [128, 32], F32))
                bb_n = P(nc.sbuf_tensor("bb_n", [128, 32], F32))
                rn_b = P(nc.sbuf_tensor("rn_b", [128, 32], F32))
                m_al = P(nc.sbuf_tensor("m_al", [128, 8 * 256], BF16))
                p_s = P(nc.sbuf_tensor("p_s", [128, 8 * 128], BF16))
                pt_s = P(nc.sbuf_tensor("pt_s", [128, 8 * 128], BF16))
                pti_s = P(nc.sbuf_tensor("pti_s", [128, 8 * 128], BF16))
                asct = P(nc.sbuf_tensor("asct", [128, 8 * 128], BF16))
                nwp = P(nc.sbuf_tensor("nwp", [128, 8 * 128], BF16))
                w_cmb = P(nc.sbuf_tensor("w_cmb", [128, 8 * 128], BF16))
                s0_s = P(nc.sbuf_tensor("s0_s", [128, 8 * 128], BF16))
                pst = P(nc.sbuf_tensor("pst", [128, 8 * 128], BF16))
                o0_b = P(nc.sbuf_tensor("o0_b", [128, DM], BF16))
                gcmb = P(nc.sbuf_tensor("gcmb", [128, DM], BF16))
                ftmp = P(nc.sbuf_tensor("ftmp", [128, DM], F32))
                ftmp2 = P(nc.sbuf_tensor("ftmp2", [128, DM], F32))
                sin_s = P(nc.sbuf_tensor("sin_s", [128, 8 * 128], BF16))
                agld = P(nc.sbuf_tensor("agld", [128, 2048], BF16))
                pjt = P(nc.sbuf_tensor("pjt", [128, 8 * 128], BF16))
                qeff = P(nc.sbuf_tensor("qeff", [128, 8 * 128], BF16))
                o0ld = P(nc.sbuf_tensor("o0ld", [128, 2 * DM], BF16))
                gld = P(nc.sbuf_tensor("gld", [128, 2 * DM], BF16))

                def ldc(sy):
                    dma(sy, s_lw[:], lnwx)
                    dma(sy, s_lb[:], lnbx)
                    dma(sy, s_mu[:], mu128)
                    dma(sy, s_mls[:], ml128s)
                t_c = step("sync", [], ldc, n_dma=4)

                def ldw(g):
                    for kt in range(DKT):
                        dma(g, w_qb[:, kt * DM:(kt + 1) * DM],
                            wqb[128 * kt:128 * (kt + 1), :], "gpsimd")
                        dma(g, w_kb[:, kt * DM:(kt + 1) * DM],
                            wkb[128 * kt:128 * (kt + 1), :], "gpsimd")
                        dma(g, w_vb[:, kt * DM:(kt + 1) * DM],
                            wvb[128 * kt:128 * (kt + 1), :], "gpsimd")
                        dma(g, w_bb[:, kt * 8:(kt + 1) * 8],
                            wbb[128 * kt:128 * (kt + 1), :], "gpsimd")
                    pass
                t_w = step("gpsimd_dma", [], ldw, n_dma=4 * DKT)

                t_st = ln_stats(x_sb, st_nmu, st_rst, ftmp,
                                [("VQ", nissued["VQ"][0]),
                                 ("SQ", nissued["SQ"][0])])

                def v_ini(v):
                    v.memset(s0_s[:], 0.0)
                    return v.tensor_copy(
                        out=pst[:].rearrange("p (h d) -> p h d", d=128),
                        in_=s_id[:].unsqueeze(1).broadcast_to([128, 8, 128]))
                t_ini = step("vector", [t_c], v_ini)

                t_s0p = t_ini
                t_pp = t_ini
                t_mb_prev = t_ini
                t_su_last = t_ini
                t_g_last = t_ini
                bankt = {i: ("SQ", 0) for i in range(6)}
                t_carry = [t_ini]

                for u in range(SC):
                    def vz(v, u=u):
                        last = None
                        for i in range(4):
                            t = 4 * u + i
                            last = v.tensor_scalar(
                                out=zbfb[:, i * DM:(i + 1) * DM],
                                in0=x_sb[:, t * DM:(t + 1) * DM],
                                scalar1=st_nmu[:, t:t + 1],
                                scalar2=st_rst[:, t:t + 1],
                                op0=ALU.add, op1=ALU.mult)
                        return last
                    t_z = step("vector", [t_st] + t_carry, vz)

                    ztiles = [(zbfb[0:128, i * DM + kt * 128:
                                    i * DM + (kt + 1) * 128],
                               zcmb[0:128, kt * SCT + i * 128:
                                    kt * SCT + (i + 1) * 128],
                               None) for i in range(4) for kt in range(DKT)]
                    t_zcm = transp2(ztiles, [t_z] + t_carry)

                    pev = {}
                    for h in range(8):
                        def tpq(t, h=h):
                            for kt in range(DKT):
                                t.matmul(pb[0][0:128, 0:512],
                                         w_qb[:, kt * DM + h * 128:
                                              kt * DM + (h + 1) * 128],
                                         zcmb[:, kt * SCT:(kt + 1) * SCT],
                                         start=(kt == 0),
                                         stop=(kt == DKT - 1))
                        tk = step("tensor", [t_zcm, bankt[0], t_w], tpq)
                        ev = step("scalar", [tk], lambda s, h=h: s.copy(
                            out=q_cm[:, h * SCT:(h + 1) * SCT],
                            in_=pb[0][0:128, 0:512]))
                        bankt[0] = ev
                        pev[("q", h)] = ev
                    for h in range(8):
                        def tpk(t, h=h):
                            for kt in range(DKT):
                                t.matmul(pb[1][0:128, 0:512],
                                         w_kb[:, kt * DM + h * 128:
                                              kt * DM + (h + 1) * 128],
                                         zcmb[:, kt * SCT:(kt + 1) * SCT],
                                         start=(kt == 0),
                                         stop=(kt == DKT - 1))
                        tk = step("tensor", [t_zcm, bankt[1], t_w], tpk)
                        ev = step("scalar", [tk], lambda s, h=h: s.copy(
                            out=k_cm[:, h * SCT:(h + 1) * SCT],
                            in_=pb[1][0:128, 0:512]))
                        bankt[1] = ev
                        pev[("k", h)] = ev
                    for i in range(4):
                        for n in range(2):
                            def tpv(t, i=i, n=n):
                                for kt in range(DKT):
                                    t.matmul(pb[2 + n][0:128, 0:512],
                                             zcmb[:, kt * SCT + i * 128:
                                                  kt * SCT + (i + 1) * 128],
                                             w_vb[:, kt * DM + n * 512:
                                                  kt * DM + (n + 1) * 512],
                                             start=(kt == 0),
                                             stop=(kt == DKT - 1))
                            tk = step("tensor", [t_zcm, bankt[2 + n], t_w],
                                      tpv)
                            ev = step("scalar", [tk],
                                      lambda s, i=i, n=n: s.activation(
                                          out=v_tk[:, i * DM + n * 512:
                                                   i * DM + (n + 1) * 512],
                                          in_=pb[2 + n][0:128, 0:512],
                                          func=AF.Silu))
                            bankt[2 + n] = ev
                            pev[("v", i, n)] = ev
                    for i in range(4):
                        def tpb(t, i=i):
                            for kt in range(DKT):
                                t.matmul(pb[4][0:128, 0:8],
                                         zcmb[:, kt * SCT + i * 128:
                                              kt * SCT + (i + 1) * 128],
                                         w_bb[:, kt * 8:(kt + 1) * 8],
                                         start=(kt == 0),
                                         stop=(kt == DKT - 1))
                        tk = step("tensor", [t_zcm, bankt[4], t_w], tpb)
                        ev = step("scalar", [tk], lambda s, i=i: s.activation(
                            out=bb_t[:, i * 8:(i + 1) * 8],
                            in_=pb[4][0:128, 0:8], func=AF.Sigmoid))
                        bankt[4] = ev
                        pev[("b", i)] = ev
                    t_bb = step("vector", [pev[("b", i)] for i in range(4)],
                                lambda v: v.tensor_scalar(
                                    out=bb_n[:], in0=bb_t[:], scalar1=0.0,
                                    scalar2=-1.0, op0=ALU.add, op1=ALU.mult))

                    ktile = [(k_cm[0:128, h * SCT + i * 128:
                                   h * SCT + (i + 1) * 128],
                              k_tk[0:128, i * DM + h * 128:
                                   i * DM + (h + 1) * 128],
                              None) for i in range(4) for h in range(8)]
                    t_ktk = transp2(ktile, [pev[("k", h)] for h in range(8)])

                    def v_rn(v):
                        v.tensor_tensor(out=ftmp[:], in0=k_tk[:, 0:DM],
                                        in1=k_tk[:, 0:DM], op=ALU.mult)
                        v.tensor_reduce(
                            out=rn_b[:, 0:8],
                            in_=ftmp[:].rearrange("p (h d) -> p h d", d=128),
                            axis=AX, op=ALU.add)
                        for i in range(1, 4):
                            v.tensor_tensor(out=ftmp[:],
                                            in0=k_tk[:, i * DM:(i + 1) * DM],
                                            in1=k_tk[:, i * DM:(i + 1) * DM],
                                            op=ALU.mult)
                            v.tensor_reduce(
                                out=rn_b[:, i * 8:(i + 1) * 8],
                                in_=ftmp[:].rearrange("p (h d) -> p h d",
                                                      d=128),
                                axis=AX, op=ALU.add)
                        return v.tensor_scalar(out=rn_b[:], in0=rn_b[:],
                                               scalar1=1e-24, scalar2=None,
                                               op0=ALU.max)
                    t_k2 = step("vector", [t_ktk, t_mb_prev], v_rn)
                    t_rn = step("scalar", [t_k2], lambda s: (
                        s.activation(out=rn_b[:], in_=rn_b[:], func=AF.Ln),
                        s.activation(out=rn_b[:], in_=rn_b[:], func=AF.Exp,
                                     scale=-0.5))[-1])

                    def v_kn(v):
                        return v.tensor_tensor(
                            out=k_tk[:].rearrange("p (g d) -> p g d", d=128),
                            in0=k_tk[:].rearrange("p (g d) -> p g d", d=128),
                            in1=rn_b[:].unsqueeze(2).broadcast_to(
                                [128, 32, 128]),
                            op=ALU.mult)
                    t_kn = step("vector", [t_rn], v_kn)

                    khtile = [(k_tk[0:128, i * DM + h * 128:
                                    i * DM + (h + 1) * 128],
                               khcm[0:128, h * SCT + i * 128:
                                    h * SCT + (i + 1) * 128],
                               None) for i in range(4) for h in range(8)]
                    t_kh = transp2(khtile, [t_kn])

                    for j in range(4):
                        jj = 4 * u + j

                        def t_arf(t, j=j):
                            for h in range(8):
                                t.matmul(pb[2 + h // 4][0:128,
                                                        (h % 4) * 128:
                                                        (h % 4) * 128 + 128],
                                         khcm[:, h * SCT + j * 128:
                                              h * SCT + (j + 1) * 128],
                                         khcm[:, h * SCT + j * 128:
                                              h * SCT + (j + 1) * 128],
                                         start=True, stop=True)
                        t_ar = step("tensor", [t_kh, bankt[2], bankt[3]],
                                    t_arf)

                        def s_na(s, j=j):
                            last = None
                            for h in range(8):
                                last = s.mul(
                                    out=p_s[:, h * 128:(h + 1) * 128],
                                    in_=pb[2 + h // 4][0:128,
                                                       (h % 4) * 128:
                                                       (h % 4) * 128 + 128],
                                    mul=bb_n[:, j * 8 + h:j * 8 + h + 1])
                            return last
                        t_na = step("scalar", [t_ar, t_bb], s_na)
                        bankt[2] = t_na
                        bankt[3] = t_na

                        def v_msk(v):
                            return v.tensor_tensor(
                                out=p_s[:].rearrange("p (h c) -> p h c",
                                                     c=128),
                                in0=p_s[:].rearrange("p (h c) -> p h c",
                                                     c=128),
                                in1=s_mls[:].unsqueeze(1).broadcast_to(
                                    [128, 8, 128]),
                                op=ALU.mult)
                        t_msk = step("vector", [t_na], v_msk)

                        attile = [(p_s[0:128, h * 128:(h + 1) * 128],
                                   pt_s[0:128, h * 128:(h + 1) * 128],
                                   None) for h in range(8)]
                        t_at = transp2(attile, [t_msk])

                        def t_astf(t, j=j):
                            for h in range(8):
                                t.matmul(pb[2 + h // 4][0:128,
                                                        (h % 4) * 128:
                                                        (h % 4) * 128 + 128],
                                         khcm[:, h * SCT + j * 128:
                                              h * SCT + (j + 1) * 128],
                                         q_cm[:, h * SCT + j * 128:
                                              h * SCT + (j + 1) * 128],
                                         start=True, stop=True)
                        t_ast = step("tensor", [t_na,
                                                pev[("q", 7)], t_kh],
                                     t_astf)
                        t_astev = step("scalar", [t_ast], lambda s: (
                            s.copy(out=asct[:, 0:512],
                                   in_=pb[2][0:128, 0:512]),
                            s.copy(out=asct[:, 512:1024],
                                   in_=pb[3][0:128, 0:512]))[-1])
                        bankt[2] = t_astev
                        bankt[3] = t_astev

                        def v_astm(v):
                            return v.tensor_tensor(
                                out=asct[:].rearrange("p (h c) -> p h c",
                                                      c=128),
                                in0=asct[:].rearrange("p (h c) -> p h c",
                                                      c=128),
                                in1=s_mu[:].unsqueeze(1).broadcast_to(
                                    [128, 8, 128]),
                                op=ALU.mult)
                        t_astm = step("vector", [t_astev], v_astm)

                        def v_mb(v, j=j):
                            v.tensor_tensor(
                                out=m_al[:].rearrange(
                                    "p (h c) -> p h c", c=256)[:, :, 0:128],
                                in0=k_tk[:, j * DM:(j + 1) * DM].rearrange(
                                    "p (h d) -> p h d", d=128),
                                in1=bb_t[:, j * 8:(j + 1) * 8].unsqueeze(2)
                                .broadcast_to([128, 8, 128]),
                                op=ALU.mult)
                            return v.tensor_tensor(
                                out=m_al[:].rearrange(
                                    "p (h c) -> p h c", c=256)[:, :, 128:256],
                                in0=v_tk[:, j * DM:(j + 1) * DM].rearrange(
                                    "p (h d) -> p h d", d=128),
                                in1=bb_t[:, j * 8:(j + 1) * 8].unsqueeze(2)
                                .broadcast_to([128, 8, 128]),
                                op=ALU.mult)
                        t_mb = step("vector", [t_kn, t_bb, t_su_last], v_mb)
                        t_mb_prev = t_mb

                        tw_e = {0: t_at, 1: t_at}
                        tw_t = {0: t_at, 1: t_at}
                        for it in range(7):
                            for w in range(2):
                                def t_serf(t, it=it, w=w):
                                    for q in range(4):
                                        h = 4 * w + q
                                        hs = slice(h * 128, (h + 1) * 128)
                                        if it <= 5:
                                            t.matmul(
                                                pb[3 * w][0:128,
                                                          q * 128:
                                                          q * 128 + 128],
                                                pt_s[:, hs], p_s[:, hs],
                                                start=True, stop=True)
                                        t.matmul(
                                            pb[3 * w + 1 + q // 2][
                                                0:128,
                                                (q % 2) * 256:
                                                (q % 2) * 256 + 256],
                                            pt_s[:, hs],
                                            m_al[:, h * 256:(h + 1) * 256],
                                            start=True, stop=False)
                                        t.matmul(
                                            pb[3 * w + 1 + q // 2][
                                                0:128,
                                                (q % 2) * 256:
                                                (q % 2) * 256 + 256],
                                            s_id[:],
                                            m_al[:, h * 256:(h + 1) * 256],
                                            start=False, stop=True)
                                t_sw = step("tensor",
                                            [t_mb, tw_e[w], tw_t[w],
                                             bankt[3 * w],
                                             bankt[3 * w + 1],
                                             bankt[3 * w + 2]], t_serf)

                                def s_sev(v, it=it, w=w):
                                    ws = slice(4 * w * 128,
                                               (4 * w + 4) * 128)
                                    if it <= 5:
                                        v.tensor_copy(
                                            out=p_s[:, ws],
                                            in_=pb[3 * w][0:128, 0:512])
                                    v.tensor_copy(
                                        out=m_al[:, 4 * w * 256:
                                                 (4 * w + 2) * 256],
                                        in_=pb[3 * w + 1][0:128, 0:512])
                                    return v.tensor_copy(
                                        out=m_al[:, (4 * w + 2) * 256:
                                                 (4 * w + 4) * 256],
                                        in_=pb[3 * w + 2][0:128, 0:512])
                                t_se = step("vector", [t_sw], s_sev)
                                tw_e[w] = t_se
                                bankt[3 * w] = t_se
                                bankt[3 * w + 1] = t_se
                                bankt[3 * w + 2] = t_se
                                if it <= 5:
                                    sttile = [
                                        (p_s[0:128,
                                             (4 * w + q) * 128:
                                             (4 * w + q + 1) * 128],
                                         pt_s[0:128,
                                              (4 * w + q) * 128:
                                              (4 * w + q + 1) * 128],
                                         None) for q in range(4)]
                                    tw_t[w] = transp2(sttile, [t_se])
                        t_ser = [tw_e[0], tw_e[1]]

                        wtile = [(m_al[0:128, h * 256:h * 256 + 128],
                                  w_cmb[0:128, h * 128:(h + 1) * 128],
                                  None) for h in range(8)]
                        t_wt = transp2(wtile, t_ser)

                        def t_wsf(t):
                            for h in range(8):
                                t.matmul(pb[h // 4][0:128,
                                                    (h % 4) * 128:
                                                    (h % 4) * 128 + 128],
                                         w_cmb[:, h * 128:(h + 1) * 128],
                                         s0_s[:, h * 128:(h + 1) * 128],
                                         start=True, stop=True)
                        t_ws = step("tensor", [t_wt, t_s0p, bankt[0],
                                               bankt[1]], t_wsf)
                        t_wse = step("vector", [t_ws], lambda v: (
                            v.tensor_scalar(out=ftmp[:, 0:512],
                                            in0=pb[0][0:128, 0:512],
                                            scalar1=0.0, scalar2=-1.0,
                                            op0=ALU.add, op1=ALU.mult),
                            v.tensor_scalar(out=ftmp[:, 512:1024],
                                            in0=pb[1][0:128, 0:512],
                                            scalar1=0.0, scalar2=-1.0,
                                            op0=ALU.add, op1=ALU.mult))[-1])
                        bankt[0] = t_wse
                        bankt[1] = t_wse

                        def v_uc(v):
                            return v.tensor_tensor(
                                out=m_al[:].rearrange(
                                    "p (h c) -> p h c", c=256)[:, :, 128:256],
                                in0=m_al[:].rearrange(
                                    "p (h c) -> p h c", c=256)[:, :, 128:256],
                                in1=ftmp[:].rearrange("p (h d) -> p h d",
                                                      d=128),
                                op=ALU.add)
                        t_uc = step("vector", [t_wse], v_uc)

                        def t_o0f(t, j=j):
                            for h in range(8):
                                bk = pb[h // 4]
                                cs = slice((h % 4) * 128, (h % 4) * 128 + 128)
                                t.matmul(bk[0:128, cs],
                                         q_cm[:, h * SCT + j * 128:
                                              h * SCT + (j + 1) * 128],
                                         s0_s[:, h * 128:(h + 1) * 128],
                                         start=True, stop=False)
                                t.matmul(bk[0:128, cs],
                                         asct[:, h * 128:(h + 1) * 128],
                                         m_al[:, h * 256 + 128:
                                              h * 256 + 256],
                                         start=False, stop=True)
                        t_o0 = step("tensor", [t_uc, t_astm, t_s0p,
                                               bankt[0], bankt[1]],
                                    t_o0f)
                        t_o0e = step("vector", [t_o0], lambda v: (
                            v.tensor_copy(out=o0_b[:, 0:512],
                                          in_=pb[0][0:128, 0:512]),
                            v.tensor_copy(out=o0_b[:, 512:1024],
                                          in_=pb[1][0:128, 0:512]))[-1])
                        bankt[0] = t_o0e
                        bankt[1] = t_o0e
                        t_o0d = step("sync", [t_o0e], lambda sy, jj=jj: dma(
                            sy, o0scr.ap()[:, jj * DM:(jj + 1) * DM],
                            o0_b[:]), n_dma=1)

                        def t_wpf(t):
                            for h in range(8):
                                t.matmul(pb[2 + h // 4][0:128,
                                                        (h % 4) * 128:
                                                        (h % 4) * 128 + 128],
                                         w_cmb[:, h * 128:(h + 1) * 128],
                                         pst[:, h * 128:(h + 1) * 128],
                                         start=True, stop=True)
                        t_wp = step("tensor", [t_wt, t_pp, bankt[2],
                                               bankt[3]], t_wpf)
                        t_wpe = step("vector", [t_wp], lambda v: (
                            v.tensor_scalar(out=nwp[:, 0:512],
                                            in0=pb[2][0:128, 0:512],
                                            scalar1=0.0, scalar2=-1.0,
                                            op0=ALU.add, op1=ALU.mult),
                            v.tensor_scalar(out=nwp[:, 512:1024],
                                            in0=pb[3][0:128, 0:512],
                                            scalar1=0.0, scalar2=-1.0,
                                            op0=ALU.add, op1=ALU.mult))[-1])
                        bankt[2] = t_wpe
                        bankt[3] = t_wpe

                        def t_gf(t, j=j):
                            for h in range(8):
                                bk = pb[4 + h // 4]
                                cs = slice((h % 4) * 128, (h % 4) * 128 + 128)
                                t.matmul(bk[0:128, cs],
                                         pst[:, h * 128:(h + 1) * 128],
                                         q_cm[:, h * SCT + j * 128:
                                              h * SCT + (j + 1) * 128],
                                         start=True, stop=False)
                                t.matmul(bk[0:128, cs],
                                         nwp[:, h * 128:(h + 1) * 128],
                                         asct[:, h * 128:(h + 1) * 128],
                                         start=False, stop=True)
                        t_g = step("tensor", [t_wpe, t_astm, t_pp,
                                              bankt[4], bankt[5]], t_gf)
                        t_ge = step("vector", [t_g], lambda v: (
                            v.tensor_copy(out=gcmb[:, 0:512],
                                          in_=pb[4][0:128, 0:512]),
                            v.tensor_copy(out=gcmb[:, 512:1024],
                                          in_=pb[5][0:128, 0:512]))[-1])
                        bankt[4] = t_ge
                        bankt[5] = t_ge
                        t_g_last = t_ge
                        step("sync", [t_ge], lambda sy, jj=jj: dma(
                            sy, gscr.ap()[:, jj * DM:(jj + 1) * DM],
                            gcmb[:]), n_dma=1)

                        def t_puf(t, j=j):
                            for h in range(8):
                                bk = pb[2 + h // 4]
                                cs = slice((h % 4) * 128, (h % 4) * 128 + 128)
                                t.matmul(bk[0:128, cs],
                                         k_tk[:, j * DM + h * 128:
                                              j * DM + (h + 1) * 128],
                                         nwp[:, h * 128:(h + 1) * 128],
                                         start=True, stop=False)
                                t.matmul(bk[0:128, cs], s_id[:],
                                         pst[:, h * 128:(h + 1) * 128],
                                         start=False, stop=True)
                        t_pu = step("tensor", [t_wpe, bankt[2], bankt[3],
                                               t_g], t_puf)

                        def t_suf(t, j=j):
                            for h in range(8):
                                bk = pb[h // 4]
                                cs = slice((h % 4) * 128, (h % 4) * 128 + 128)
                                t.matmul(bk[0:128, cs],
                                         k_tk[:, j * DM + h * 128:
                                              j * DM + (h + 1) * 128],
                                         m_al[:, h * 256 + 128:
                                              h * 256 + 256],
                                         start=True, stop=False)
                                t.matmul(bk[0:128, cs], s_id[:],
                                         s0_s[:, h * 128:(h + 1) * 128],
                                         start=False, stop=True)
                        t_su = step("tensor", [t_pu, t_o0, bankt[0],
                                               bankt[1]], t_suf)
                        t_su_last = t_su
                        t_sue = step("vector", [t_su], lambda v: (
                            v.tensor_copy(out=pst[:, 0:512],
                                          in_=pb[2][0:128, 0:512]),
                            v.tensor_copy(out=pst[:, 512:1024],
                                          in_=pb[3][0:128, 0:512]),
                            v.tensor_copy(out=s0_s[:, 0:512],
                                          in_=pb[0][0:128, 0:512]),
                            v.tensor_copy(out=s0_s[:, 512:1024],
                                          in_=pb[1][0:128, 0:512]))[-1])
                        bankt[0] = t_sue
                        bankt[1] = t_sue
                        bankt[2] = t_sue
                        bankt[3] = t_sue
                        t_s0p = t_sue
                        t_pev = t_sue
                        t_pp = t_sue

                    t_carry = [t_g_last, t_sue, t_pev, t_mb]

                def agw(sy):
                    dma(sy, ag2_i.ap()[:, 0:1024], pst[:])
                    dma(sy, ag2_i.ap()[:, 1024:2048], s0_s[:])
                t_agw = step("sync", [t_pp, t_s0p], agw, n_dma=2)

                def dcc2(g):
                    g.collective_compute(
                        "AllGather", ALU.bypass,
                        replica_groups=[list(range(8))],
                        ins=[ag2_i.ap()], outs=[ag2_o.ap()],
                    ).then_inc(sems["CC"][0], 1)
                step("gpsimd", [t_agw, ("GD", sems["GD"][1])], dcc2)
                sems["CC"][1] += 1
                t_ag2 = ("CC", sems["CC"][1])

                t_z0 = step("vector", [t_ag2],
                            lambda v: v.memset(sin_s[:], 0.0))
                t_fold = t_z0
                for jr in range(8):
                    t_lj = step("sync", [t_fold], lambda sy, jr=jr: dma(
                        sy, agld[:],
                        ag2_o.ap()[128 * jr:128 * (jr + 1), :]), n_dma=1)
                    pjtile = [(agld[0:128, h * 128:(h + 1) * 128],
                               pjt[0:128, h * 128:(h + 1) * 128],
                               None) for h in range(8)]
                    t_pjt = transp2(pjtile, [t_lj, t_fold])

                    def v_me(v, jr=jr):
                        v.tensor_scalar(out=pjt[:], in0=pjt[:],
                                        scalar1=0.0,
                                        scalar2=s_pm[:, jr:jr + 1],
                                        op0=ALU.add, op1=ALU.mult)
                        v.scalar_tensor_tensor(
                            out=pjt[:].rearrange("p (h c) -> p h c", c=128),
                            in0=s_id[:].unsqueeze(1).broadcast_to(
                                [128, 8, 128]),
                            scalar=s_pmi[:, jr:jr + 1],
                            in1=pjt[:].rearrange("p (h c) -> p h c", c=128),
                            op0=ALU.mult, op1=ALU.add)
                        return v.tensor_scalar(
                            out=qeff[:], in0=agld[:, 1024:2048],
                            scalar1=0.0, scalar2=s_pm[:, jr:jr + 1],
                            op0=ALU.add, op1=ALU.mult)
                    t_me = step("vector", [t_pjt], v_me)

                    def t_fmf(t):
                        for h in range(8):
                            bk = pb[h // 4]
                            cs = slice((h % 4) * 128, (h % 4) * 128 + 128)
                            t.matmul(bk[0:128, cs],
                                     pjt[:, h * 128:(h + 1) * 128],
                                     sin_s[:, h * 128:(h + 1) * 128],
                                     start=True, stop=False)
                            t.matmul(bk[0:128, cs], s_id[:],
                                     qeff[:, h * 128:(h + 1) * 128],
                                     start=False, stop=True)
                    t_fm = step("tensor", [t_me, bankt[0], bankt[1]], t_fmf)
                    t_fe = step("scalar", [t_fm], lambda s: (
                        s.copy(out=sin_s[:, 0:512],
                               in_=pb[0][0:128, 0:512]),
                        s.copy(out=sin_s[:, 512:1024],
                               in_=pb[1][0:128, 0:512]))[-1])
                    bankt[0] = t_fe
                    bankt[1] = t_fe
                    t_fold = t_fe

                t_ld = ("SD", sems["SD"][1])
                t_app = t_fold
                t_prev_app = {0: t_fold, 1: t_fold}
                for jj in range(TT):
                    pr = jj % 2
                    b0, b1 = (0, 1) if pr == 0 else (2, 3)

                    def ldo(sy, jj=jj, pr=pr):
                        dma(sy, o0ld[:, pr * DM:(pr + 1) * DM],
                            o0scr.ap()[:, jj * DM:(jj + 1) * DM])
                        dma(sy, gld[:, pr * DM:(pr + 1) * DM],
                            gscr.ap()[:, jj * DM:(jj + 1) * DM])
                    t_lo = step("sync", [t_prev_app[pr], t_ld], ldo, n_dma=2)
                    t_ld = t_lo

                    def t_ocf(t, pr=pr, b0=b0, b1=b1):
                        for h in range(8):
                            bk = pb[b0 + h // 4]
                            cs = slice((h % 4) * 128, (h % 4) * 128 + 128)
                            t.matmul(bk[0:128, cs],
                                     gld[:, pr * DM + h * 128:
                                         pr * DM + (h + 1) * 128],
                                     sin_s[:, h * 128:(h + 1) * 128],
                                     start=True, stop=False)
                            t.matmul(bk[0:128, cs], s_id[:],
                                     o0ld[:, pr * DM + h * 128:
                                          pr * DM + (h + 1) * 128],
                                     start=False, stop=True)
                    t_oc = step("tensor", [t_lo, t_fold, bankt[b0],
                                           bankt[b1]], t_ocf)

                    nmu_c = st_nm2[:, jj:jj + 1]
                    rs_c = st_rs2[:, jj:jj + 1]

                    fx = ftmp if pr == 0 else ftmp2
                    t_sq = step("scalar", [t_oc, t_prev_app[pr]],
                                lambda s, b0=b0, b1=b1, fx=fx: (
                        s.activation(out=fx[:, 0:512],
                                     in_=pb[b0][0:128, 0:512],
                                     func=AF.Square),
                        s.activation(out=fx[:, 512:1024],
                                     in_=pb[b1][0:128, 0:512],
                                     func=AF.Square))[-1])

                    def v_st1(v, jj=jj, b0=b0, b1=b1, fx=fx):
                        v.tensor_reduce(out=st_t1[:, 0:1],
                                        in_=pb[b0][0:128, 0:512], axis=AX,
                                        op=ALU.add)
                        v.tensor_reduce(out=st_t1[:, 1:2],
                                        in_=pb[b1][0:128, 0:512], axis=AX,
                                        op=ALU.add)
                        v.tensor_tensor(out=st_t1[:, 0:1],
                                        in0=st_t1[:, 0:1],
                                        in1=st_t1[:, 1:2], op=ALU.add)
                        v.tensor_scalar(out=st_nm2[:, jj:jj + 1],
                                        in0=st_t1[:, 0:1], scalar1=0.0,
                                        scalar2=-1.0 / DM, op0=ALU.add,
                                        op1=ALU.mult)
                        v.tensor_reduce(out=st_t1[:, 2:3],
                                        in_=fx[:, 0:512], axis=AX,
                                        op=ALU.add)
                        v.tensor_reduce(out=st_t1[:, 3:4],
                                        in_=fx[:, 512:1024], axis=AX,
                                        op=ALU.add)
                        v.tensor_tensor(out=st_t1[:, 2:3],
                                        in0=st_t1[:, 2:3],
                                        in1=st_t1[:, 3:4], op=ALU.add)
                        v.tensor_scalar(out=st_t1[:, 2:3],
                                        in0=st_t1[:, 2:3], scalar1=0.0,
                                        scalar2=1.0 / DM, op0=ALU.add,
                                        op1=ALU.mult)
                        v.tensor_tensor(out=st_t1[:, 1:2],
                                        in0=st_nm2[:, jj:jj + 1],
                                        in1=st_nm2[:, jj:jj + 1],
                                        op=ALU.mult)
                        v.tensor_tensor(out=st_rs2[:, jj:jj + 1],
                                        in0=st_t1[:, 2:3],
                                        in1=st_t1[:, 1:2], op=ALU.subtract)
                        return v.tensor_scalar(out=st_rs2[:, jj:jj + 1],
                                               in0=st_rs2[:, jj:jj + 1],
                                               scalar1=LNEPS, scalar2=None,
                                               op0=ALU.add)
                    t_st1 = step("vector", [t_oc, t_sq, t_app], v_st1)
                    t_rsq = step("scalar", [t_st1], lambda s, rs_c=rs_c: (
                        s.activation(out=rs_c, in_=rs_c, func=AF.Ln),
                        s.activation(out=rs_c, in_=rs_c, func=AF.Exp,
                                     scale=-0.5))[-1])

                    def v_app(v, jj=jj, nmu_c=nmu_c, rs_c=rs_c,
                              b0=b0, b1=b1, fx=fx):
                        for n, bk in enumerate((b0, b1)):
                            v.tensor_scalar(
                                out=fx[:, n * 512:(n + 1) * 512],
                                in0=pb[bk][0:128, 0:512],
                                scalar1=nmu_c, scalar2=rs_c,
                                op0=ALU.add, op1=ALU.mult)
                        v.tensor_tensor(out=fx[:], in0=fx[:],
                                        in1=s_lw[:], op=ALU.mult)
                        v.tensor_tensor(out=fx[:], in0=fx[:],
                                        in1=s_lb[:], op=ALU.add)
                        xt = x_sb[:, jj * DM:(jj + 1) * DM]
                        return v.scalar_tensor_tensor(
                            out=xt, in0=fx[:], scalar=1.0, in1=xt,
                            op0=ALU.mult, op1=ALU.add)
                    t_app = step("vector", [t_rsq], v_app)
                    t_prev_app[pr] = t_app
                    bankt[b0] = t_app
                    bankt[b1] = t_app

            barrier()

        emit_ffn()
        if STAGE == 2:
            emit_epilogue(dbg=DBG_APS)
            flush_prog()
            return nc

        emit_dnb_done = False
        if STAGE >= 3:
            emit_dnb_done = True
            emit_attn_b()

        if STAGE >= 4:
            emit_ffn("2")

        emit_epilogue(dbg=DBG_APS)
        flush_prog()
    return nc


_CACHE = {}


def kernel(**inputs):
    x = np.ascontiguousarray(inputs["L0_attn_a_x"], np.float32)
    B, T, D = x.shape
    spans = []
    for c in range(8):
        b, s = c // 4, c % 4
        spans.append(np.ascontiguousarray(x[b, s * NT:(s + 1) * NT]))

    f32 = lambda a: np.ascontiguousarray(a, np.float32)
    ag = lambda a: np.ascontiguousarray(a)

    triu = np.triu(np.ones((64, 64), np.float32))
    tril1 = np.tril(np.ones((64, 64), np.float32), -1)
    rep = lambda m: np.vstack([m, m]).astype(np.float32)
    half = int(np.float32(0.5).view(np.uint32))
    icon = np.tile(np.array([[0x1FFFFF, 0xFFC00000 - (1 << 32), -(1 << 31),
                              0x7FFFFFFF, half, 0, 0, 0]], np.int64)
                   .astype(np.int32), (128, 1))
    scanm_v = np.ones((128, SCT), np.float32)
    scanm_v[:, ::CH] = 0.0
    gam = f32(inputs["L0_attn_a_gamma"]).reshape(1, DM)
    bet = f32(inputs["L0_attn_a_beta"]).reshape(1, DM)
    lnw = f32(inputs["L0_attn_b_ln_w"]).reshape(1, DM)
    lnb = f32(inputs["L0_attn_b_ln_b"]).reshape(1, DM)
    ba = f32(inputs["L0_attn_a_ba"]).reshape(4, 128).T

    base = dict(
        wqa=f32(inputs["L0_attn_a_Wq"]), wka=f32(inputs["L0_attn_a_Wk"]),
        wva=f32(inputs["L0_attn_a_Wv"]), waa=f32(inputs["L0_attn_a_Wa"]),
        baa=ag(ba), woa=f32(inputs["L0_attn_a_Wo"]),
        gamx=np.broadcast_to(gam, (128, DM)).copy(),
        betx=np.broadcast_to(bet, (128, DM)).copy(),
        wqb=f32(inputs["L0_attn_b_Wq"]), wkb=f32(inputs["L0_attn_b_Wk"]),
        wvb=f32(inputs["L0_attn_b_Wv"]), wbb=f32(inputs["L0_attn_b_Wb"]),
        lnwx=np.broadcast_to(lnw, (128, DM)).copy(),
        lnbx=np.broadcast_to(lnb, (128, DM)).copy(),
        ident=np.eye(128, dtype=BF), i64b=rep(np.eye(64)).astype(BF),
        m_triu=rep(triu), m_tril1=rep(tril1),
        mu128=np.triu(np.ones((128, 128), np.float32)),
        ml128s=np.tril(np.ones((128, 128), np.float32), -1),
        scanm=scanm_v, ones8=np.ones((128, 8), np.float32), iconst=icon,
    )
    wg = f32(inputs["L0_ffn_w_gate"])
    wu = f32(inputs["L0_ffn_w_up"])
    wd = f32(inputs["L0_ffn_w_down"])

    key = "nc"
    if key not in _CACHE:
        _CACHE[key] = build()
    nc = _CACHE[key]

    in_maps = []
    for c in range(8):
        b, s = c // 4, c % 4
        pmv = np.zeros((128, 8), np.float32)
        for j in range(4 * b, 4 * b + s):
            pmv[:, j] = 1.0
        d = dict(base)
        d["x_in"] = spans[c]
        d["pm"] = pmv
        d["pmi"] = 1.0 - pmv
        d["wg_sl"] = np.ascontiguousarray(wg[:, 128 * c:128 * (c + 1)])
        d["wu_sl"] = np.ascontiguousarray(wu[:, 128 * c:128 * (c + 1)])
        wds = [wd[:, 128 * t:128 * (t + 1)] for t in
               (c, c + 8, min(c + 16, 21))]
        d["wd_sl"] = np.ascontiguousarray(np.concatenate(wds, axis=1))
        in_maps.append(d)

    res = run_bass_kernel_spmd(nc, in_maps, core_ids=list(range(8)))
    out = np.empty((B, T, D), np.float32)
    for c in range(8):
        b, s = c // 4, c % 4
        out[b, s * NT:(s + 1) * NT] = res.results[c]["x_out"]
    return out


# version 09 (final)
# speedup vs baseline: 1.0366x; 1.0366x over previous
import contextlib
import os
import numpy as np
import ml_dtypes
import concourse.bass as bass
import concourse.mybir as mybir
from concourse.bass_utils import run_bass_kernel_spmd

F32 = mybir.dt.float32
BF16 = mybir.dt.bfloat16
I32 = mybir.dt.int32
ALU = mybir.AluOpType
AF = mybir.ActivationFunctionType
AX = mybir.AxisListType.X
BF = ml_dtypes.bfloat16

STAGE = int(os.environ.get('KSTAGE', '4'))
NT, TT, SC, SCT, CH = 2048, 16, 4, 512, 64
DM, DKT, ICH, ITL = 1024, 8, 2816, 22
LNEPS = 1e-5
SUB = int(os.environ.get('KSUB', '99'))
SUBB = int(os.environ.get('KSUBB', '99'))
NOAG = int(os.environ.get('KNOAG', '0'))
KEVAC = int(os.environ.get('KEVAC', '1'))
KTRN = int(os.environ.get('KTRN', '9999'))
KSRC = int(os.environ.get('KSRC', '0'))
KPROJ = int(os.environ.get('KPROJ', '4'))
ENGS = ("sync", "gpsimd", "vector", "scalar", "tensor")


def build():
    nc = bass.Bass("TRN2", target_bir_lowering=False,
                   detect_race_conditions=False,
                   dynamic_dma_scratch_size=16384)

    def din(name, shape, dt=F32):
        return nc.dram_tensor(name, shape, dt, kind="ExternalInput").ap()

    x_in = din("x_in", [NT, DM])
    wqa = din("wqa", [DM, 512]); wka = din("wka", [DM, 512])
    wva = din("wva", [DM, 512]); waa = din("waa", [DM, 512])
    baa = din("baa", [128, 4])
    gamx = din("gamx", [128, DM]); betx = din("betx", [128, DM])
    woa = din("woa", [512, DM])
    wqb = din("wqb", [DM, DM]); wkb = din("wkb", [DM, DM])
    wvb = din("wvb", [DM, DM]); wbb = din("wbb", [DM, 8])
    lnwx = din("lnwx", [128, DM]); lnbx = din("lnbx", [128, DM])
    wg_sl = din("wg_sl", [ICH, 128]); wu_sl = din("wu_sl", [ICH, 128])
    wd_sl = din("wd_sl", [DM, 3 * 128])
    ident = din("ident", [128, 128], BF16)
    m_triu = din("m_triu", [128, 64])
    m_tril1 = din("m_tril1", [128, 64])
    mu128 = din("mu128", [128, 128])
    ml128s = din("ml128s", [128, 128])
    i64b = din("i64b", [128, 64], BF16)
    scanm = din("scanm", [128, SCT])
    ones8 = din("ones8", [128, 8])
    iconst = din("iconst", [128, 8], I32)
    pm = din("pm", [128, 8]); pmi = din("pmi", [128, 8])
    x_out = nc.dram_tensor("x_out", [NT, DM], F32, kind="ExternalOutput").ap()
    dbg_out = nc.dram_tensor("dbg_out", [128, 2048], F32, kind="ExternalOutput").ap()

    ag0_i = nc.dram_tensor("ag0_i", [128, 2 * ICH + 3 * DM], BF16)
    ag0_o = nc.dram_tensor("ag0_o", [8 * 128, 2 * ICH + 3 * DM], BF16)
    ag1_i = nc.dram_tensor("ag1_i", [128, 4 * 64 + 4], BF16)
    ag1_o = nc.dram_tensor("ag1_o", [8 * 128, 4 * 64 + 4], BF16)
    ag2_i = nc.dram_tensor("ag2_i", [128, 16 * 128], BF16)
    ag2_o = nc.dram_tensor("ag2_o", [8 * 128, 16 * 128], BF16)
    o0scr = nc.dram_tensor("o0scr", [128, DKT * NT], BF16)
    gscr = nc.dram_tensor("gscr", [128, DKT * NT], BF16)

    es = contextlib.ExitStack()
    with es:
        E = es.enter_context
        x_sb = E(nc.sbuf_tensor("x_sb", [128, TT * DM], BF16))
        s_id = E(nc.sbuf_tensor("s_id", [128, 128], BF16))
        s_mtu = E(nc.sbuf_tensor("s_mtu", [128, 64], F32))
        s_mt1 = E(nc.sbuf_tensor("s_mt1", [128, 64], F32))
        s_i64 = E(nc.sbuf_tensor("s_i64", [128, 64], BF16))
        s_scm = E(nc.sbuf_tensor("s_scm", [128, SCT], F32))
        s_on8 = E(nc.sbuf_tensor("s_on8", [128, 8], F32))
        s_ic = E(nc.sbuf_tensor("s_ic", [128, 8], I32))
        s_pm = E(nc.sbuf_tensor("s_pm", [128, 8], F32))
        s_pmi = E(nc.sbuf_tensor("s_pmi", [128, 8], F32))
        s_baa = E(nc.sbuf_tensor("s_baa", [128, 4], F32))
        st_nmu = E(nc.sbuf_tensor("st_nmu", [128, TT], F32))
        st_rst = E(nc.sbuf_tensor("st_rst", [128, TT], F32))
        st_nm2 = E(nc.sbuf_tensor("st_nm2", [128, TT], F32))
        st_rs2 = E(nc.sbuf_tensor("st_rs2", [128, TT], F32))
        st_sum = E(nc.sbuf_tensor("st_sum", [128, TT], F32))
        st_ssq = E(nc.sbuf_tensor("st_ssq", [128, TT], F32))
        st_t1 = E(nc.sbuf_tensor("st_t1", [128, TT], F32))
        st_t2 = E(nc.sbuf_tensor("st_t2", [128, TT], F32))
        st_snap = E(nc.sbuf_tensor("st_snap", [128, 6 * TT], F32))
        pb = [E(nc.psum_tensor(f"pb{i}", [128, 512], F32)) for i in range(6)]
        pt = [E(nc.psum_tensor(f"pt{i}", [128, 512], BF16)) for i in range(2)]

        sems = {nm: [E(nc.semaphore(nm)), 0]
                for nm in ("SD", "GD", "CC", "PH", "TQ", "VQ", "SQ", "GQ")}
        blk = E(nc.Block())

        QSEM = {"sync": "SD", "gpsimd_dma": "GD", "gpsimd": "GQ",
                "vector": "VQ", "scalar": "SQ", "tensor": "TQ"}
        BUDGET = {"TQ": int(os.environ.get("KBT", "3")), "VQ": int(os.environ.get("KBV", "1")), "SQ": int(os.environ.get("KBS", "2")), "GQ": int(os.environ.get("KBG", "1"))}
        GATED = {
            "tensor_tensor", "tensor_scalar", "tensor_copy", "tensor_reduce",
            "reduce_max", "reduce_sum", "memset", "memzero", "reciprocal",
            "tensor_tensor_scan", "scalar_tensor_tensor", "select", "iota",
            "activation", "copy", "add", "mul", "sign", "square",
            "tensor_add", "tensor_sub", "tensor_mul", "tensor_max",
            "tensor_scalar_add", "tensor_scalar_sub", "tensor_scalar_mul",
            "tensor_scalar_max", "tensor_scalar_min", "tensor_single_scalar",
            "matmul", "transpose",
        }
        nissued = {k: [0] for k in BUDGET}
        emitted = {k: [0] for k in BUDGET}

        class EWrap:
            def __init__(self, e, qn):
                self._e = e
                self._qn = qn

            def __getattr__(self, name):
                f = getattr(self._e, name)
                if name not in GATED:
                    return f
                qn = self._qn
                sem = sems[qn][0]
                cnt = emitted[qn]
                bud = BUDGET[qn]

                def g(*a, **kw):
                    if cnt[0] >= bud:
                        self._e.wait_ge(sem, cnt[0] - bud + 1)
                    inst = f(*a, **kw)
                    inst.then_inc(sem, 1)
                    cnt[0] += 1
                    return inst
                return g

        class _DummyInst:
            def then_inc(self, *a, **kw):
                return self

            def branch_hint(self, *a, **kw):
                return self

        _DUMMY = _DummyInst()

        class CountStub:
            def __init__(self, box):
                self._box = box

            def __getattr__(self, name):
                if name in GATED:
                    box = self._box

                    def g(*a, **kw):
                        box[0] += 1
                        return _DUMMY
                    return g
                return lambda *a, **kw: _DUMMY

        PROG = {e: [] for e in ("sync", "gpsimd", "vector", "scalar",
                                "tensor")}

        def flush_prog():
            for beng in PROG:
                fns = PROG[beng]
                if not fns:
                    continue
                PROG[beng] = []

                def mega(e, fns=fns):
                    for fn in fns:
                        fn(e)
                getattr(blk, beng)(mega)

        def step(eng, deps, f, n_dma=0):
            qn = QSEM[eng]
            beng = "gpsimd" if eng == "gpsimd_dma" else eng
            gated_engine = qn in BUDGET and eng != "gpsimd_dma"

            def _(e):
                seen = {}
                for dn, dv in deps:
                    if dv > seen.get(dn, 0):
                        seen[dn] = dv
                for dn, dv in seen.items():
                    e.wait_ge(sems[dn][0], dv)
                if gated_engine:
                    n0 = emitted[qn][0]
                    f(EWrap(e, qn))
                    if emitted[qn][0] == n0:
                        e.nop().then_inc(sems[qn][0], 1)
                        emitted[qn][0] += 1
                    return
                f(e)
            PROG[beng].append(_)
            if gated_engine:
                box = [0]
                f(CountStub(box))
                nissued[qn][0] += max(box[0], 1)
                return (qn, nissued[qn][0])
            sems[qn][1] += 16 * n_dma
            return (qn, sems[qn][1])

        def dma(e, out, in_, eng="sync"):
            sem = sems["SD" if eng == "sync" else "GD"]
            e.dma_start(out=out, in_=in_).then_inc(sem[0], 16)

        def barrier():
            tgt = sems["PH"][1] + 5
            for eng in ENGS:
                def _(e, tgt=tgt):
                    e.nop().then_inc(sems["PH"][0], 1)
                    e.wait_ge(sems["PH"][0], tgt)
                PROG[eng].append(_)
            sems["PH"][1] = tgt

        def quant16(src, dst, cols, r6, sc_, za, zb, deps):
            nb = cols // 16

            def v1(v):
                v.tensor_reduce(out=r6[:, 0:nb],
                                in_=src.rearrange("p (b k) -> p b k", k=16),
                                axis=AX, op=ALU.max, apply_absolute_value=True)
                v.tensor_scalar(out=r6[:, 0:nb], in0=r6[:, 0:nb],
                                scalar1=6e-12, scalar2=None, op0=ALU.max)
                v.tensor_scalar(out=sc_[:, 0:nb], in0=r6[:, 0:nb], scalar1=0.0,
                                scalar2=1.0 / 6.0, op0=ALU.add, op1=ALU.mult)
                v.reciprocal(out=r6[:, 0:nb], in_=r6[:, 0:nb])
                v.tensor_scalar(out=r6[:, 0:nb], in0=r6[:, 0:nb], scalar1=0.0,
                                scalar2=6.0, op0=ALU.add, op1=ALU.mult)
                v.tensor_tensor(
                    out=za.rearrange("p (b k) -> p b k", k=16),
                    in0=src.rearrange("p (b k) -> p b k", k=16),
                    in1=r6[:, 0:nb].unsqueeze(2).broadcast_to([128, nb, 16]),
                    op=ALU.mult)
                zi = za.bitcast(I32)
                bi = zb.bitcast(I32)
                v.tensor_scalar(out=bi, in0=zi, scalar1=s_ic[:, 3:4],
                                scalar2=None, op0=ALU.bitwise_and)
                v.tensor_scalar(out=dst, in0=zb,
                                scalar1=0.25, scalar2=1.0,
                                op0=ALU.is_gt, op1=ALU.mult)
                return v.tensor_scalar(out=zb, in0=zb,
                                       scalar1=0.5, scalar2=None, op0=ALU.max)
            tv = step("vector", deps, v1)

            def g1(g):
                bi = zb.bitcast(I32)
                return g.tensor_tensor(
                    out=bi, in0=bi,
                    in1=s_ic[:, 0:1].broadcast_to([128, cols]), op=ALU.add)
            tg = step("gpsimd", [tv], g1)

            def v2(v):
                zi = za.bitcast(I32)
                bi = zb.bitcast(I32)
                v.tensor_scalar(out=bi, in0=bi, scalar1=s_ic[:, 1:2],
                                scalar2=None, op0=ALU.bitwise_and)
                v.tensor_scalar(out=zi, in0=zi, scalar1=s_ic[:, 2:3],
                                scalar2=None, op0=ALU.bitwise_and)
                v.tensor_tensor(out=bi, in0=bi, in1=zi, op=ALU.bitwise_or)
                v.tensor_tensor(out=zb, in0=zb,
                                in1=dst, op=ALU.mult)
                return v.tensor_tensor(
                    out=dst.rearrange("p (b k) -> p b k", k=16),
                    in0=zb.rearrange("p (b k) -> p b k", k=16),
                    in1=sc_[:, 0:nb].unsqueeze(2).broadcast_to([128, nb, 16]),
                    op=ALU.mult)
            return step("vector", [tg], v2)

        _tp_last = [("SQ", 0)]

        def transp2(tiles, deps):
            evs = []
            tiles = tiles[:KTRN]
            if not tiles:
                return deps[0] if deps else ("VQ", 0)
            deps = list(deps) + [_tp_last[0]]
            for i, (src, dst, ef) in enumerate(tiles):
                bank = pt[i % 2]
                p, w = src.shape[0], src.shape[1]
                dd = list(deps) + ([evs[i - 2]] if i >= 2 else [])
                ttk = step("tensor", dd,
                           lambda t, src=src, w=w, p=p, bank=bank:
                           t.transpose(bank[0:w, 0:p], src, s_id[0:p, 0:p]))

                def sf(s, dst=dst, w=w, p=p, bank=bank, ef=ef):
                    if not KEVAC:
                        return None
                    if ef is None:
                        return s.copy(out=dst, in_=bank[0:w, 0:p])
                    return ef(s, bank[0:w, 0:p], dst)
                ev = step("scalar", [ttk], sf)
                evs.append(ev)
            _tp_last[0] = evs[-1]
            return evs[-1]

        def ln_stats(src_sb, dst_nmu, dst_rst, tmp, deps, dbg=0):
            def v1(v):
                for t in range(TT):
                    xt = src_sb[:, t * DM:(t + 1) * DM]
                    v.tensor_reduce(out=st_sum[:, t:t + 1], in_=xt, axis=AX,
                                    op=ALU.add)
                    v.tensor_tensor(out=tmp[:, 0:DM], in0=xt, in1=xt,
                                    op=ALU.mult)
                    v.tensor_reduce(out=st_ssq[:, t:t + 1], in_=tmp[:, 0:DM],
                                    axis=AX, op=ALU.add)
                v.tensor_scalar(out=dst_nmu[:], in0=st_sum[:], scalar1=0.0,
                                scalar2=-1.0 / DM, op0=ALU.add, op1=ALU.mult)
                v.tensor_scalar(out=st_ssq[:], in0=st_ssq[:], scalar1=0.0,
                                scalar2=1.0 / DM, op0=ALU.add, op1=ALU.mult)
                v.tensor_tensor(out=dst_rst[:], in0=dst_nmu[:], in1=dst_nmu[:],
                                op=ALU.mult)
                v.tensor_tensor(out=dst_rst[:], in0=st_ssq[:], in1=dst_rst[:],
                                op=ALU.subtract)
                return v.tensor_scalar(out=dst_rst[:], in0=dst_rst[:],
                                       scalar1=LNEPS, scalar2=None, op0=ALU.add)
            tv = step("vector", deps, v1)
            if dbg:
                tv = step("vector", [tv], lambda v: v.tensor_copy(
                    out=st_snap[:, 4 * TT:5 * TT], in_=dst_rst[:]))

            def s1(s):
                s.activation(out=dst_rst[:], in_=dst_rst[:], func=AF.Ln)
                return s.activation(out=dst_rst[:], in_=dst_rst[:],
                                    func=AF.Exp, scale=-0.5)
            ts1 = step("scalar", [tv], s1)
            if dbg:
                ts1 = step("vector", [ts1], lambda v: v.tensor_copy(
                    out=st_snap[:, 5 * TT:6 * TT], in_=dst_rst[:]))
            return ts1

        wa_stack = contextlib.ExitStack()
        WA = wa_stack.enter_context
        w_qa = WA(nc.sbuf_tensor("w_qa", [128, DKT * 512], BF16))
        w_ka = WA(nc.sbuf_tensor("w_ka", [128, DKT * 512], BF16))
        w_va = WA(nc.sbuf_tensor("w_va", [128, DKT * 512], BF16))
        w_aa = WA(nc.sbuf_tensor("w_aa", [128, DKT * 512], BF16))
        w_oa = WA(nc.sbuf_tensor("w_oa", [128, 4 * DM], BF16))
        s_gx = WA(nc.sbuf_tensor("s_gx", [128, DM], F32))
        s_bx = WA(nc.sbuf_tensor("s_bx", [128, DM], F32))

        with contextlib.ExitStack() as p0:
            P0 = p0.enter_context
            fw32 = P0(nc.sbuf_tensor("fw32", [128, 24 * 128], F32))
            fq16 = P0(nc.sbuf_tensor("fq16", [128, 24 * 128], BF16))
            fsh = P0(nc.sbuf_tensor("fsh", [128, 24 * 128], BF16))
            qza = P0(nc.sbuf_tensor("qza", [128, 24 * 128], F32))
            qzb = P0(nc.sbuf_tensor("qzb", [128, 24 * 128], F32))
            qr6 = P0(nc.sbuf_tensor("qr6", [128, 24 * 8], F32))
            qsc = P0(nc.sbuf_tensor("qsc", [128, 24 * 8], F32))

            def d0(sy):
                for dst, src in ((s_id, ident), (s_mtu, m_triu),
                                 (s_mt1, m_tril1), (s_i64, i64b),
                                 (s_scm, scanm), (s_ic, iconst), (s_pm, pm),
                                 (s_pmi, pmi), (s_baa, baa), (s_gx, gamx),
                                 (s_bx, betx), (s_on8, ones8)):
                    dma(sy, dst[:], src)
            t_con = step("sync", [], d0, n_dma=12)

            def d1(g):
                for t in range(TT):
                    dma(g, x_sb[:, t * DM:(t + 1) * DM],
                        x_in[128 * t:128 * (t + 1), :], "gpsimd")
                for dst, src in ((w_qa, wqa), (w_ka, wka), (w_va, wva),
                                 (w_aa, waa)):
                    for kt in range(DKT):
                        dma(g, dst[:, kt * 512:(kt + 1) * 512],
                            src[128 * kt:128 * (kt + 1), :], "gpsimd")
                for kt in range(4):
                    dma(g, w_oa[:, kt * DM:(kt + 1) * DM],
                        woa[128 * kt:128 * (kt + 1), :], "gpsimd")
            t_xw = step("gpsimd_dma", [], d1, n_dma=TT + 4 * DKT + 4)

            t_prev = t_con
            tout_prev = None
            ttr_prev = None
            offs = [(wg_sl, 0, ITL), (wu_sl, ICH, ITL), (wd_sl, 2 * ICH, 24)]
            for si, (src_h, dst_off, ntl) in enumerate(offs):
                cols = ntl * 128 if si < 2 else 3 * DKT * 128
                def ld(sy, src_h=src_h, cols=cols):
                    nrow = src_h.shape[0] // 128
                    ncol = src_h.shape[1] // 128
                    for s in range(ncol):
                        for kt in range(nrow):
                            j = s * nrow + kt
                            dma(sy, fw32[:, j * 128:(j + 1) * 128],
                                src_h[128 * kt:128 * (kt + 1),
                                      128 * s:128 * (s + 1)])
                tl = step("sync", [t_prev], ld, n_dma=cols // 128)
                tq = quant16(fw32[0:128, 0:cols], fq16[0:128, 0:cols], cols,
                             qr6, qsc, qza[0:128, 0:cols], qzb[0:128, 0:cols],
                             [tl, t_con] + ([ttr_prev] if ttr_prev else []))
                n_tr = cols // 128
                tiles = [(fq16[0:128, i * 128:(i + 1) * 128],
                          fsh[0:128, i * 128:(i + 1) * 128], None)
                         for i in range(n_tr)]
                ttr = transp2(tiles, [tq, t_con] +
                              ([tout_prev] if tout_prev else []))
                tout = step("gpsimd_dma", [ttr],
                            lambda g, dst_off=dst_off, cols=cols:
                            dma(g, ag0_i.ap()[:, dst_off:dst_off + cols],
                                fsh[:, 0:cols], "gpsimd"), n_dma=1)
                tout_prev = tout
                ttr_prev = ttr
                t_prev = tq
                if si == 0 and os.environ.get("KDW"):
                    def d0(g):
                        dma(g, dbg_out[:, 0:256], fw32[:, 2560:2816], "gpsimd")
                        dma(g, dbg_out[:, 256:512], fq16[:, 2560:2816],
                            "gpsimd")
                        dma(g, dbg_out[:, 512:768], fsh[:, 2560:2816],
                            "gpsimd")
                    t_prev = step("gpsimd_dma", [ttr], d0, n_dma=3)
            t_lastout = tout

            def dcc(g):
                if NOAG:
                    return None
                g.collective_compute(
                    "AllGather", ALU.bypass, replica_groups=[list(range(8))],
                    ins=[ag0_i.ap()], outs=[ag0_o.ap()],
                ).then_inc(sems["CC"][0], 1)
            step("gpsimd", [t_lastout, ("GD", sems["GD"][1])], dcc)
            if not NOAG:
                sems["CC"][1] += 1
            t_ag0 = ("CC", sems["CC"][1])

            barrier()

        DBG_APS = []
        if STAGE >= 1:
            with contextlib.ExitStack() as pa:
                P = pa.enter_context
                f32a = P(nc.sbuf_tensor("f32a", [128, 4 * DM], F32))
                f32b = P(nc.sbuf_tensor("f32b", [128, 4 * SCT], F32))
                rel_f = P(nc.sbuf_tensor("rel_f", [128, 4 * SCT], F32))
                exp_f = P(nc.sbuf_tensor("exp_f", [128, 4 * SCT], F32))
                zbf = P(nc.sbuf_tensor("zbf", [128, 4 * DM], BF16))
                zcm = P(nc.sbuf_tensor("zcm", [128, DKT * SCT], BF16))
                qcm = P(nc.sbuf_tensor("qcm", [128, 4 * SCT], BF16))
                kcm = P(nc.sbuf_tensor("kcm", [128, 4 * SCT], BF16))
                qt_ = P(nc.sbuf_tensor("qt_", [128, 4 * SCT], BF16))
                kt_ = P(nc.sbuf_tensor("kt_", [128, 4 * SCT], BF16))
                kh_ = P(nc.sbuf_tensor("kh_", [128, 4 * SCT], BF16))
                vtk = P(nc.sbuf_tensor("vtk", [128, 4 * 512], BF16))
                ktk = P(nc.sbuf_tensor("ktk", [128, 4 * 512], BF16))
                rnt = P(nc.sbuf_tensor("rnt", [128, 4 * 8], F32))
                scmb = P(nc.sbuf_tensor("scmb", [128, 8 * 64], BF16))
                khtb = P(nc.sbuf_tensor("khtb", [128, 8 * 64], BF16))
                s0g = P(nc.sbuf_tensor("s0g", [128, 4 * 64], BF16))
                pref = P(nc.sbuf_tensor("pref", [128, 4], F32))
                endb = P(nc.sbuf_tensor("endb", [128, 4 * 8], F32))
                o0a = P(nc.sbuf_tensor("o0a", [128, 4 * NT], BF16))
                qht = P(nc.sbuf_tensor("qht", [128, 4 * NT], BF16))
                agb = P(nc.sbuf_tensor("agb", [128, 4 * 64 + 4], BF16))
                sing = P(nc.sbuf_tensor("sing", [128, 4 * 64], BF16))
                aef = P(nc.sbuf_tensor("aef", [128, 4], F32))
                aef2 = P(nc.sbuf_tensor("aef2", [128, 64], F32))

                t_st = ln_stats(x_sb, st_nmu, st_rst, f32a, [t_xw], dbg=1)

                def v_st2(v):
                    v.tensor_scalar(out=st_nm2[:], in0=st_sum[:], scalar1=0.0,
                                    scalar2=1.0 / DM, op0=ALU.add, op1=ALU.mult)
                    v.tensor_tensor(out=st_nm2[:], in0=st_nm2[:], in1=st_nmu[:],
                                    op=ALU.add)
                    v.tensor_tensor(out=st_nm2[:], in0=st_nm2[:], in1=st_rst[:],
                                    op=ALU.mult)
                    v.tensor_tensor(out=st_t1[:], in0=st_nmu[:], in1=st_sum[:],
                                    op=ALU.mult)
                    v.tensor_scalar(out=st_t1[:], in0=st_t1[:], scalar1=0.0,
                                    scalar2=2.0 / DM, op0=ALU.add, op1=ALU.mult)
                    v.tensor_tensor(out=st_t2[:], in0=st_nmu[:], in1=st_nmu[:],
                                    op=ALU.mult)
                    v.tensor_tensor(out=st_t1[:], in0=st_t1[:], in1=st_t2[:],
                                    op=ALU.add)
                    v.tensor_tensor(out=st_t1[:], in0=st_t1[:], in1=st_ssq[:],
                                    op=ALU.add)
                    v.tensor_tensor(out=st_t2[:], in0=st_rst[:], in1=st_rst[:],
                                    op=ALU.mult)
                    v.tensor_tensor(out=st_t1[:], in0=st_t1[:], in1=st_t2[:],
                                    op=ALU.mult)
                    v.tensor_tensor(out=st_t2[:], in0=st_nm2[:], in1=st_nm2[:],
                                    op=ALU.mult)
                    v.tensor_tensor(out=st_rs2[:], in0=st_t1[:], in1=st_t2[:],
                                    op=ALU.subtract)
                    v.tensor_scalar(out=st_rs2[:], in0=st_rs2[:], scalar1=LNEPS,
                                    scalar2=None, op0=ALU.add)
                    return v.tensor_scalar(out=st_nm2[:], in0=st_nm2[:],
                                           scalar1=0.0, scalar2=-1.0,
                                           op0=ALU.add, op1=ALU.mult)
                tv2 = step("vector", [t_st], v_st2)

                t_st2 = step("scalar", [tv2], lambda s: (
                    s.activation(out=st_rs2[:], in_=st_rs2[:], func=AF.Ln),
                    s.activation(out=st_rs2[:], in_=st_rs2[:], func=AF.Exp,
                                 scale=-0.5))[-1])

                def v_snap(v):
                    v.tensor_copy(out=st_snap[:, 0:TT], in_=st_nmu[:])
                    v.tensor_copy(out=st_snap[:, TT:2 * TT], in_=st_rst[:])
                    v.tensor_copy(out=st_snap[:, 2 * TT:3 * TT], in_=st_nm2[:])
                    return v.tensor_copy(out=st_snap[:, 3 * TT:4 * TT],
                                         in_=st_rs2[:])
                t_snap = step("vector", [t_st2], v_snap)

                t_init = step("vector", [], lambda v: (
                    v.memset(s0g[:], 0.0), v.memset(pref[:], 0.0),
                    v.memset(o0a[:], 0.0), v.memset(qht[:], 0.0))[-1])

                bank_free = {i: ("VQ", 0) for i in range(6)}
                t_sc_ev = ("VQ", 0)
                t_o_ev = ("VQ", 0)
                t_kt_ev = ("VQ", 0)
                t_kt_sc_prev = ("SQ", 0)
                t_s_ev = t_init
                for u in range(SC if SUB >= 4 else (1 if SUB >= 2 else 0)):
                    def vz(v, u=u):
                        for i in range(4):
                            t = 4 * u + i
                            xt = x_sb[:, t * DM:(t + 1) * DM]
                            sl = slice(i * DM, (i + 1) * DM)
                            v.tensor_scalar(out=f32a[:, sl], in0=xt,
                                            scalar1=st_nmu[:, t:t + 1],
                                            scalar2=st_rst[:, t:t + 1],
                                            op0=ALU.add, op1=ALU.mult)
                            v.tensor_scalar(out=f32a[:, sl], in0=f32a[:, sl],
                                            scalar1=st_nm2[:, t:t + 1],
                                            scalar2=st_rs2[:, t:t + 1],
                                            op0=ALU.add, op1=ALU.mult)
                            v.tensor_tensor(out=f32a[:, sl], in0=f32a[:, sl],
                                            in1=s_gx[:], op=ALU.mult)
                            v.tensor_tensor(out=zbf[:, sl], in0=f32a[:, sl],
                                            in1=s_bx[:], op=ALU.add)
                        return None
                    t_z = step("vector", [t_st2, t_con], vz)

                    if SUBB < 2:
                        continue
                    zsrc = x_sb if KSRC else zbf
                    ztiles = [(zsrc[0:128, i * DM + kt * 128:
                                    i * DM + (kt + 1) * 128],
                               zcm[0:128, kt * SCT + i * 128:
                                   kt * SCT + (i + 1) * 128],
                               None) for i in range(4) for kt in range(DKT)]
                    t_zcm = transp2(ztiles, [t_z])

                    if SUBB < 3:
                        continue
                    pev = {}
                    for bank, w_sb, dst, bias in (
                            (0, w_qa, qcm, None), (1, w_ka, kcm, None),
                            (2, w_aa, None, s_baa))[:KPROJ]:
                        for mt in range(4):
                            def tp(t, bank=bank, w_sb=w_sb, mt=mt):
                                for kt in range(DKT):
                                    t.matmul(pb[bank][0:128, 0:512],
                                             w_sb[:, kt * 512 + mt * 128:
                                                  kt * 512 + (mt + 1) * 128],
                                             zcm[:, kt * SCT:(kt + 1) * SCT],
                                             start=(kt == 0), stop=(kt == DKT - 1))
                            tk = step("tensor", [t_zcm, bank_free[bank], t_xw], tp)

                            def vp(s, bank=bank, dst=dst, mt=mt, bias=bias):
                                if dst is not None:
                                    return s.copy(
                                        out=dst[:, mt * SCT:(mt + 1) * SCT],
                                        in_=pb[bank][0:128, 0:512])
                                return s.add(
                                    out=f32b[:, mt * SCT:(mt + 1) * SCT],
                                    in_=pb[bank][0:128, 0:512],
                                    add=s_baa[:, mt:mt + 1])
                            ev = step("scalar", [tk], vp)
                            bank_free[bank] = ev
                            pev[(bank, mt)] = ev
                    for i in range(4 if KPROJ >= 4 else 0):
                        def tpv(t, i=i):
                            for kt in range(DKT):
                                t.matmul(pb[3][0:128, 0:512],
                                         zcm[:, kt * SCT + i * 128:
                                             kt * SCT + (i + 1) * 128],
                                         w_va[:, kt * 512:(kt + 1) * 512],
                                         start=(kt == 0), stop=(kt == DKT - 1))
                        tk = step("tensor", [t_zcm, bank_free[3], t_xw], tpv)
                        ev = step("scalar", [tk], lambda s, i=i: s.copy(
                            out=vtk[:, i * 512:(i + 1) * 512],
                            in_=pb[3][0:128, 0:512]))
                        bank_free[3] = ev
                        pev[(3, i)] = ev

                    if SUBB < 4:
                        continue
                    ktiles = [(kcm[0:128, g * SCT + i * 128:g * SCT + (i + 1) * 128],
                               ktk[0:128, i * 512 + g * 128:i * 512 + (g + 1) * 128],
                               None) for i in range(4) for g in range(4)]
                    t_ktk = transp2(ktiles, [pev[(1, 0)], pev[(1, 1)],
                                             pev[(1, 2)], pev[(1, 3)]])

                    def v_rn(v, u=u):
                        for i in range(4):
                            v.tensor_tensor(out=f32a[:, 0:512],
                                            in0=ktk[:, i * 512:(i + 1) * 512],
                                            in1=ktk[:, i * 512:(i + 1) * 512],
                                            op=ALU.mult)
                            v.tensor_reduce(
                                out=rnt[:, i * 8:(i + 1) * 8],
                                in_=f32a[:, 0:512].rearrange(
                                    "p (h d) -> p h d", d=64),
                                axis=AX, op=ALU.add)
                        return v.tensor_scalar(out=rnt[:], in0=rnt[:],
                                               scalar1=1e-24, scalar2=None,
                                               op0=ALU.max)
                    t_k2 = step("vector", [t_ktk], v_rn)
                    t_rn = step("scalar", [t_k2], lambda s: (
                        s.activation(out=rnt[:], in_=rnt[:], func=AF.Ln),
                        s.activation(out=rnt[:], in_=rnt[:], func=AF.Exp,
                                     scale=-0.5))[-1])

                    if SUBB < 5:
                        continue
                    def s_la(s):
                        s.activation(out=f32a[:, 0:4 * SCT], in_=f32b[:],
                                     func=AF.Abs)
                        s.activation(out=f32a[:, 0:4 * SCT],
                                     in_=f32a[:, 0:4 * SCT], func=AF.Exp,
                                     scale=-1.0)
                        return s.activation(out=f32a[:, 0:4 * SCT],
                                            in_=f32a[:, 0:4 * SCT], func=AF.Ln,
                                            bias=1.0)
                    tsl = step("scalar", [pev[(2, m)] for m in range(4)], s_la)

                    def v_la(v, u=u):
                        zc = s_ic[:, 5:6].bitcast(F32).broadcast_to([128, 4 * SCT])
                        v.tensor_tensor(out=f32b[:], in0=f32b[:], in1=zc,
                                        op=ALU.min)
                        v.tensor_tensor(out=f32b[:], in0=f32b[:],
                                        in1=f32a[:, 0:4 * SCT], op=ALU.subtract)
                        for g in range(4):
                            v.tensor_tensor_scan(
                                out=rel_f[:, g * SCT:(g + 1) * SCT],
                                data0=s_scm[:],
                                data1=f32b[:, g * SCT:(g + 1) * SCT],
                                initial=0.0, op0=ALU.mult, op1=ALU.add)
                        return v.tensor_copy(
                            out=endb[:].rearrange("p (g c) -> p g c", c=8),
                            in_=rel_f[:].rearrange("p (g ct) -> p g ct", g=4)
                            .rearrange("p g (c t) -> p g c t", t=CH)[:, :, :,
                                                                     CH - 1])
                    tvla = step("vector", [tsl], v_la)

                    def v_pr(v, u=u):
                        for g in range(4):
                            v.tensor_tensor_scan(
                                out=f32a[:, g * 8:(g + 1) * 8], data0=s_on8[:],
                                data1=endb[:, g * 8:(g + 1) * 8], initial=0.0,
                                op0=ALU.mult, op1=ALU.add)
                        v.tensor_tensor(out=f32a[:, 0:32], in0=f32a[:, 0:32],
                                        in1=endb[:], op=ALU.subtract)
                        for g in range(4):
                            v.tensor_scalar(out=f32a[:, g * 8:(g + 1) * 8],
                                            in0=f32a[:, g * 8:(g + 1) * 8],
                                            scalar1=pref[:, g:g + 1],
                                            scalar2=None, op0=ALU.add)
                        for g in range(4):
                            v.tensor_reduce(out=aef[:, g:g + 1],
                                            in_=endb[:, g * 8:(g + 1) * 8],
                                            axis=AX, op=ALU.add)
                        v.tensor_tensor(out=pref[:], in0=pref[:], in1=aef[:],
                                        op=ALU.add)
                        return v.tensor_tensor(
                            out=f32b[:].rearrange("p (gc t) -> p gc t", t=CH),
                            in0=rel_f[:].rearrange("p (gc t) -> p gc t", t=CH),
                            in1=f32a[:, 0:32].unsqueeze(2).broadcast_to(
                                [128, 32, CH]),
                            op=ALU.add)
                    tvpr = step("vector", [tvla], v_pr)

                    if SUBB < 6:
                        continue
                    t_e1 = step("scalar", [tvpr], lambda s: s.activation(
                        out=exp_f[:], in_=rel_f[:], func=AF.Exp))
                    t_q1 = step("vector", [t_e1, pev[(0, 0)], pev[(0, 1)],
                                           pev[(0, 2)], pev[(0, 3)]],
                                lambda v: v.tensor_tensor(out=qt_[:], in0=qcm[:],
                                                          in1=exp_f[:],
                                                          op=ALU.mult))
                    t_ae = step("vector", [t_e1], lambda v: v.tensor_copy(
                        out=endb[:].rearrange("p (g c) -> p g c", c=8),
                        in_=exp_f[:].rearrange("p (g ct) -> p g ct", g=4)
                        .rearrange("p g (c t) -> p g c t", t=CH)[:, :, :,
                                                                 CH - 1]))
                    t_e2 = step("scalar", [t_q1, t_ae], lambda s: s.activation(
                        out=exp_f[:], in_=rel_f[:], func=AF.Exp, scale=-1.0))
                    t_k1 = step("vector", [t_e2], lambda v: v.tensor_tensor(
                        out=kt_[:], in0=kcm[:], in1=exp_f[:], op=ALU.mult))
                    def v_e3in(v):
                        return v.tensor_tensor(
                            out=exp_f[:].rearrange("p (gc t) -> p gc t", t=CH),
                            in0=rel_f[:].rearrange(
                                "p (gc t) -> p gc t", t=CH)[:, :, CH - 1:CH]
                            .broadcast_to([128, 32, CH]),
                            in1=rel_f[:].rearrange("p (gc t) -> p gc t", t=CH),
                            op=ALU.subtract)
                    t_e3i = step("vector", [t_k1], v_e3in)
                    t_e3 = step("scalar", [t_e3i], lambda s: s.activation(
                        out=exp_f[:], in_=exp_f[:], func=AF.Exp))
                    t_k2b = step("vector", [t_e3], lambda v: v.tensor_tensor(
                        out=kh_[:], in0=kcm[:], in1=exp_f[:], op=ALU.mult))
                    t_e4 = step("scalar", [t_k2b], lambda s: s.activation(
                        out=f32b[:], in_=f32b[:], func=AF.Exp))
                    t_qh = step("vector", [t_e4], lambda v, u=u: v.tensor_tensor(
                        out=qht[:].rearrange("p (g n) -> p g n", n=NT)[
                            :, :, u * SCT:(u + 1) * SCT],
                        in0=qcm[:].rearrange("p (g n) -> p g n", n=SCT),
                        in1=f32b[:].rearrange("p (g n) -> p g n", n=SCT),
                        op=ALU.mult))

                    for c in range(8 if SUB >= 3 else 0):
                        tc, off = c // 2, 64 * (c % 2)
                        cs = slice(c * CH, (c + 1) * CH)

                        def t_sc(t, u=u, c=c, off=off):
                            for h in range(8):
                                g, ho = h // 2, 64 * (h % 2)
                                t.matmul(pb[0][off:off + 64, 64 * h:64 * h + 64],
                                         kt_[ho:ho + 64, g * SCT + c * CH:
                                             g * SCT + (c + 1) * CH],
                                         qt_[ho:ho + 64, g * SCT + c * CH:
                                             g * SCT + (c + 1) * CH],
                                         start=True, stop=True)
                        tk_sc = step("tensor", [t_q1, t_k1, t_sc_ev], t_sc)

                        def v_sc(v, c=c, tc=tc, off=off):
                            v.tensor_tensor(
                                out=scmb[off:off + 64, :].rearrange(
                                    "p (h t) -> p h t", t=64),
                                in0=pb[0][off:off + 64, :].rearrange(
                                    "p (h t) -> p h t", t=64),
                                in1=rnt[off:off + 64, tc * 8:(tc + 1) * 8]
                                .unsqueeze(2).broadcast_to([64, 8, 64]),
                                op=ALU.mult)
                            return v.tensor_tensor(
                                out=scmb[off:off + 64, :].rearrange(
                                    "p (h t) -> p h t", t=64),
                                in0=scmb[off:off + 64, :].rearrange(
                                    "p (h t) -> p h t", t=64),
                                in1=s_mtu[off:off + 64, :].unsqueeze(1)
                                .broadcast_to([64, 8, 64]),
                                op=ALU.mult)
                        t_sc_ev = step("vector", [tk_sc, t_rn], v_sc)

                        def t_o(t, u=u, c=c, tc=tc, off=off):
                            for h in range(8):
                                g, ho = h // 2, 64 * (h % 2)
                                t.matmul(pb[1][ho:ho + 64, g * 64:(g + 1) * 64],
                                         vtk[off:off + 64, tc * 512 + 64 * h:
                                             tc * 512 + 64 * h + 64],
                                         scmb[off:off + 64, 64 * h:64 * h + 64],
                                         start=True, stop=False)
                                t.matmul(pb[1][ho:ho + 64, g * 64:(g + 1) * 64],
                                         s0g[ho:ho + 64, g * 64:(g + 1) * 64],
                                         qt_[ho:ho + 64, g * SCT + c * CH:
                                             g * SCT + (c + 1) * CH],
                                         start=False, stop=True)
                        tk_o = step("tensor", [t_sc_ev, t_s_ev, t_o_ev,
                                               pev[(3, tc)]], t_o)

                        def v_o(v, u=u, c=c):
                            return v.tensor_copy(
                                out=o0a[:].rearrange("p (g n) -> p g n", n=NT)[
                                    :, :, u * SCT + c * CH:u * SCT + (c + 1) * CH],
                                in_=pb[1][0:128, 0:256].rearrange(
                                    "p (g t) -> p g t", t=64))
                        t_o_ev = step("vector", [tk_o], v_o)

                        def t_kt(t, c=c, off=off):
                            for h in range(8):
                                g, ho = h // 2, 64 * (h % 2)
                                t.transpose(pt[0][off:off + 64, 64 * h:64 * h + 64],
                                            kh_[ho:ho + 64, g * SCT + c * CH:
                                                g * SCT + (c + 1) * CH],
                                            s_i64[ho:ho + 64, :])
                        tk_kt = step("tensor", [t_k2b, t_o_ev, t_kt_sc_prev],
                                     t_kt)

                        def s_kt(s, c=c, off=off):
                            return s.copy(out=khtb[off:off + 64, :],
                                          in_=pt[0][off:off + 64, 0:512])
                        t_kt_sc = step("scalar", [tk_kt], s_kt)

                        def v_kt(v, c=c, tc=tc, off=off):
                            return v.tensor_tensor(
                                out=khtb[off:off + 64, :].rearrange(
                                    "p (h j) -> p h j", j=64),
                                in0=khtb[off:off + 64, :].rearrange(
                                    "p (h j) -> p h j", j=64),
                                in1=rnt[off:off + 64, tc * 8:(tc + 1) * 8]
                                .unsqueeze(2).broadcast_to([64, 8, 64]),
                                op=ALU.mult)
                        t_kt_ev = step("vector", [t_kt_sc], v_kt)
                        t_kt_sc_prev = t_kt_sc

                        def t_su(t, c=c, tc=tc, off=off):
                            for h in range(8):
                                g, ho = h // 2, 64 * (h % 2)
                                t.matmul(pb[2][ho:ho + 64, g * 64:(g + 1) * 64],
                                         khtb[off:off + 64, 64 * h:64 * h + 64],
                                         vtk[off:off + 64, tc * 512 + 64 * h:
                                             tc * 512 + 64 * h + 64],
                                         start=True, stop=True)
                        tk_su = step("tensor", [t_kt_ev, t_s_ev], t_su)

                        def v_su(v, u=u, c=c):
                            last = None
                            for h in range(8):
                                g, ho = h // 2, 64 * (h % 2)
                                last = v.scalar_tensor_tensor(
                                    out=s0g[ho:ho + 64, g * 64:(g + 1) * 64],
                                    in0=s0g[ho:ho + 64, g * 64:(g + 1) * 64],
                                    scalar=endb[ho:ho + 64, g * 8 + c:g * 8 + c + 1],
                                    in1=pb[2][ho:ho + 64, g * 64:(g + 1) * 64],
                                    op0=ALU.mult, op1=ALU.add)
                            return last
                        t_s_ev = step("vector", [tk_su, t_ae], v_su)

                if SUB >= 5:
                    def v_agb(v):
                        return v.tensor_copy(out=agb[:, 0:256], in_=s0g[:])
                    t_ab = step("vector", [t_s_ev], v_agb)
                    t_ap = step("scalar", [t_ab], lambda s: s.activation(
                        out=agb[:, 256:260], in_=pref[:], func=AF.Exp))
                    t_agd = step("gpsimd_dma", [t_ap], lambda g: dma(
                        g, ag1_i.ap()[:], agb[:], "gpsimd"), n_dma=1)

                    def dcc1(g):
                        g.collective_compute(
                            "AllGather", ALU.bypass, replica_groups=[list(range(8))],
                            ins=[ag1_i.ap()], outs=[ag1_o.ap()],
                        ).then_inc(sems["CC"][0], 1)
                    step("gpsimd", [t_agd, ("GD", sems["GD"][1])], dcc1)
                    sems["CC"][1] += 1
                    t_ag1 = ("CC", sems["CC"][1])

                    t_z0 = step("vector", [t_ag1], lambda v: v.memset(sing[:], 0.0))
                    t_fold = t_z0
                    for j in range(8):
                        t_lj = step("gpsimd_dma", [t_fold], lambda g, j=j: dma(
                            g, agb[:], ag1_o.ap()[128 * j:128 * (j + 1), :],
                            "gpsimd"), n_dma=1)

                        def v_fold(v, j=j):
                            v.tensor_scalar(out=aef[:], in0=agb[:, 256:260],
                                            scalar1=0.0, scalar2=s_pm[:, j:j + 1],
                                            op0=ALU.add, op1=ALU.mult)
                            v.tensor_tensor(out=aef[:], in0=aef[:],
                                            in1=s_pmi[:, j:j + 1].broadcast_to(
                                                [128, 4]), op=ALU.add)
                            last = None
                            for g in range(4):
                                v.tensor_scalar(out=sing[:, g * 64:(g + 1) * 64],
                                                in0=sing[:, g * 64:(g + 1) * 64],
                                                scalar1=0.0,
                                                scalar2=aef[:, g:g + 1],
                                                op0=ALU.add, op1=ALU.mult)
                                v.tensor_scalar(
                                    out=aef2[:, 0:64],
                                    in0=agb[:, g * 64:(g + 1) * 64],
                                    scalar1=0.0, scalar2=s_pm[:, j:j + 1],
                                    op0=ALU.add, op1=ALU.mult)
                                last = v.tensor_tensor(
                                    out=sing[:, g * 64:(g + 1) * 64],
                                    in0=sing[:, g * 64:(g + 1) * 64],
                                    in1=aef2[:, 0:64], op=ALU.add)
                            return last
                        t_fold = step("vector", [t_lj], v_fold)

            DBG_APS.extend([(st_sum[:], 16), (st_ssq[:], 16),
                            (st_rst[:], 16), (st_rs2[:], 16),
                            (st_snap[:], 96)])
            if SUB < 5:
                t_fold = t_init
            if SUB >= 6:
                t_oc_pp = {0: t_fold, 1: t_fold}
                t_oc_ev = t_fold
                for idx in range(SC * 4):
                    u, gp = idx // 4, idx % 4
                    pr = idx % 2
                    obk = 4 + pr

                    def t_ocr(t, u=u, gp=gp, obk=obk):
                        for half in range(2):
                            ho = 64 * half
                            t.matmul(pb[obk][ho:ho + 64, 0:512],
                                     sing[ho:ho + 64,
                                          gp * 64:(gp + 1) * 64],
                                     qht[ho:ho + 64, gp * NT + u * SCT:
                                         gp * NT + (u + 1) * SCT],
                                     start=True, stop=True)
                    tk = step("tensor", [t_oc_pp[pr], t_fold], t_ocr)

                    def v_ocr(v, u=u, gp=gp, obk=obk):
                        return v.scalar_tensor_tensor(
                            out=o0a[:, gp * NT + u * SCT:
                                    gp * NT + (u + 1) * SCT],
                            in0=pb[obk][0:128, 0:512], scalar=1.0,
                            in1=o0a[:, gp * NT + u * SCT:
                                    gp * NT + (u + 1) * SCT],
                            op0=ALU.mult, op1=ALU.add)
                    t_oc_ev = step("vector", [tk], v_ocr)
                    t_oc_pp[pr] = t_oc_ev
            else:
                t_oc_ev = t_fold

            if SUB >= 7:
                t_xpp = {0: t_oc_ev, 1: t_oc_ev}
                t_xev = t_oc_ev
                for tt in range(TT):
                    pr = tt % 2
                    xb0 = 2 * pr

                    def t_op(t, tt=tt, xb0=xb0):
                        for g in range(4):
                            for n in range(2):
                                t.matmul(pb[xb0 + n][0:128, 0:512],
                                         o0a[:, g * NT + tt * 128:
                                             g * NT + (tt + 1) * 128],
                                         w_oa[:, g * DM + n * 512:
                                              g * DM + (n + 1) * 512],
                                         start=(g == 0), stop=(g == 3))
                    tk = step("tensor", [t_xpp[pr], t_oc_ev], t_op)

                    def v_xr(v, tt=tt, xb0=xb0, pr=pr):
                        xt = x_sb[:, tt * DM:(tt + 1) * DM]
                        fbuf = f32a if pr == 0 else f32b
                        v.tensor_scalar(out=fbuf[:, 0:DM], in0=xt,
                                        scalar1=st_nmu[:, tt:tt + 1],
                                        scalar2=st_rst[:, tt:tt + 1],
                                        op0=ALU.add, op1=ALU.mult)
                        last = None
                        for n in range(2):
                            v.tensor_tensor(
                                out=fbuf[:, n * 512:(n + 1) * 512],
                                in0=fbuf[:, n * 512:(n + 1) * 512],
                                in1=pb[xb0 + n][0:128, 0:512], op=ALU.add)
                            last = v.tensor_tensor(
                                out=xt[0:128, n * 512:(n + 1) * 512],
                                in0=fbuf[:, n * 512:(n + 1) * 512],
                                in1=xt[0:128, n * 512:(n + 1) * 512],
                                op=ALU.add)
                        return last
                    t_xev = step("vector", [tk], v_xr)
                    t_xpp[pr] = t_xev

            barrier()

        def emit_epilogue(dbg=()):
            with contextlib.ExitStack() as pe:
                stg = pe.enter_context(
                    nc.sbuf_tensor("stg", [128, 2 * DM], F32))
                dbuf = pe.enter_context(
                    nc.sbuf_tensor("dbuf", [128, 2048], F32))
                def vdbg(v):
                    if os.environ.get("KDW"):
                        return v.memset(dbuf[0:1, 0:1], 0.0)
                    v.memset(dbuf[:], 0.0)
                    col = 0
                    for ap_, w in dbg:
                        v.tensor_copy(out=dbuf[:, col:col + w], in_=ap_)
                        col += w
                    return None
                tvd = step("vector", [("VQ", nissued["VQ"][0]),
                                      ("SQ", nissued["SQ"][0]),
                                      ("GQ", nissued["GQ"][0]),
                                      ("GD", sems["GD"][1])], vdbg)
                if not (os.environ.get("KDW") or os.environ.get("KDW2")):
                    step("sync", [tvd], lambda sy: dma(sy, dbg_out, dbuf[:]),
                         n_dma=1)
                tprev = ("VQ", nissued["VQ"][0])
                td = ("SD", sems["SD"][1])
                for t in range(TT):
                    tv = step("vector", [tprev, td], lambda v, t=t:
                              v.tensor_copy(
                                  out=stg[:, (t % 2) * DM:(t % 2 + 1) * DM],
                                  in_=x_sb[:, t * DM:(t + 1) * DM]))
                    td = step("sync", [tv], lambda sy, t=t: dma(
                        sy, x_out.rearrange("(t p) d -> p t d", p=128)[
                            :, t, :],
                        stg[:, (t % 2) * DM:(t % 2 + 1) * DM]), n_dma=1)
                    tprev = tv
                step("sync", [td], lambda sy: None)
                step("gpsimd", [("CC", sems["CC"][1])], lambda g: None)

        if STAGE == 1:
            emit_epilogue(dbg=DBG_APS)
            flush_prog()
            return nc
        wa_stack.close()

        ICCH = [(0, 512), (512, 512), (1024, 512), (1536, 512),
                (2048, 512), (2560, 256)]

        def emit_ffn(tag=""):
            with contextlib.ExitStack() as pf:
                P = pf.enter_context
                w_g = P(nc.sbuf_tensor("w_g" + tag, [128, DKT * ICH], BF16))
                w_u = P(nc.sbuf_tensor("w_u" + tag, [128, DKT * ICH], BF16))
                w_d = P(nc.sbuf_tensor("w_d" + tag, [128, ITL * DM], BF16))
                fb1 = P(nc.sbuf_tensor("fb1" + tag, [128, DM], F32))
                fzb = P(nc.sbuf_tensor("fzb" + tag, [128, 1408], F32))
                fr6 = P(nc.sbuf_tensor("fr6" + tag, [128, 64], F32))
                fsc = P(nc.sbuf_tensor("fsc" + tag, [128, 64], F32))
                fxq = P(nc.sbuf_tensor("fxq" + tag, [128, DM], BF16))
                fxqc = P(nc.sbuf_tensor("fxqc" + tag, [128, DKT * 128], BF16))
                fsg = P(nc.sbuf_tensor("fsg" + tag, [128, 512], F32))
                fha = P(nc.sbuf_tensor("fha" + tag, [128, ICH], F32))
                fhr6 = P(nc.sbuf_tensor("fhr6" + tag, [128, 88], F32))
                fhsc = P(nc.sbuf_tensor("fhsc" + tag, [128, 88], F32))
                fhq = P(nc.sbuf_tensor("fhq" + tag, [128, ICH], BF16))
                fhqc = P(nc.sbuf_tensor("fhqc" + tag, [128, 11 * 128],
                                        BF16))
                fdbg = None
                if os.environ.get("KDW2"):
                    fdbg = P(nc.sbuf_tensor("fdbg" + tag, [128, 1024], F32))

                def ldwg(sy):
                    for r in range(DKT):
                        slab = ag0_o.ap()[128 * r:128 * (r + 1), :]
                        dma(sy, w_g[:, r * ICH:(r + 1) * ICH], slab[:, 0:ICH])
                t_wg = step("sync", [t_ag0], ldwg, n_dma=DKT)

                def ldwu(sy):
                    for r in range(DKT):
                        slab = ag0_o.ap()[128 * r:128 * (r + 1), :]
                        dma(sy, w_u[:, r * ICH:(r + 1) * ICH],
                            slab[:, ICH:2 * ICH])
                t_wu = step("sync", [t_wg], ldwu, n_dma=DKT)

                def ldwd(sy):
                    for t in range(ITL):
                        rank, slot = t % 8, t // 8
                        slab = ag0_o.ap()[128 * rank:128 * (rank + 1), :]
                        dma(sy, w_d[:, t * DM:(t + 1) * DM],
                            slab[:, 2 * ICH + slot * DM:
                                 2 * ICH + (slot + 1) * DM])
                t_w = step("sync", [t_wu], ldwd, n_dma=ITL)
                if os.environ.get("KDW"):
                    def dwdbg(g):
                        dma(g, dbg_out[:, 768:1024], w_g[:, 2560:2816],
                            "gpsimd")
                        dma(g, dbg_out[:, 1024:1280],
                            ag0_o.ap()[0:128, 2560:2816], "gpsimd")
                        dma(g, dbg_out[:, 1280:1536], w_u[:, 2560:2816],
                            "gpsimd")
                        dma(g, dbg_out[:, 1536:1792],
                            w_d[:, 21 * DM + 768:21 * DM + 1024], "gpsimd")
                        dma(g, dbg_out[:, 1792:2048],
                            w_g[:, 7 * ICH + 2560:7 * ICH + 2816], "gpsimd")
                    step("gpsimd_dma", [t_w], dwdbg, n_dma=5)

                t_st = ln_stats(x_sb, st_nmu, st_rst, fb1,
                                [("VQ", nissued["VQ"][0]),
                                 ("SQ", nissued["SQ"][0])])
                t_res_prev = t_st
                bank_gu = {0: ("VQ", 0), 1: ("VQ", 0)}
                bank_d = ("VQ", 0)
                t_xc_prev = ("SQ", 0)
                t_vh_prev = ("VQ", 0)
                t_hc_prev = ("SQ", 0)
                for tt in range(TT):
                    def vx(v, tt=tt):
                        return v.tensor_scalar(
                            out=fb1[:], in0=x_sb[:, tt * DM:(tt + 1) * DM],
                            scalar1=st_nmu[:, tt:tt + 1],
                            scalar2=st_rst[:, tt:tt + 1],
                            op0=ALU.add, op1=ALU.mult)
                    t_x = step("vector", [t_st], vx)
                    t_q = quant16(fb1[0:128, 0:DM], fxq[0:128, 0:DM], DM,
                                  fr6, fsc, fb1[0:128, 0:DM],
                                  fzb[0:128, 0:DM], [t_x, t_xc_prev])
                    xt_tiles = [(fxq[0:128, k * 128:(k + 1) * 128],
                                 fxqc[0:128, k * 128:(k + 1) * 128], None)
                                for k in range(DKT)]
                    t_xc = transp2(xt_tiles, [t_q])
                    t_xc_prev = t_xc

                    t_h_ch = {}
                    for j, (ic0, icw) in enumerate(ICCH):
                        par = j % 2
                        gb0, gb1 = (0, 1) if par == 0 else (2, 3)

                        def tgu(t, ic0=ic0, icw=icw, gb0=gb0, gb1=gb1):
                            for kt in range(DKT):
                                t.matmul(pb[gb0][0:128, 0:icw],
                                         fxqc[:, kt * 128:(kt + 1) * 128],
                                         w_g[:, kt * ICH + ic0:
                                             kt * ICH + ic0 + icw],
                                         start=(kt == 0), stop=(kt == DKT - 1))
                            for kt in range(DKT):
                                t.matmul(pb[gb1][0:128, 0:icw],
                                         fxqc[:, kt * 128:(kt + 1) * 128],
                                         w_u[:, kt * ICH + ic0:
                                             kt * ICH + ic0 + icw],
                                         start=(kt == 0), stop=(kt == DKT - 1))
                        tk = step("tensor", [t_xc, bank_gu[par], t_wu], tgu)

                        def ssi(s, icw=icw, gb0=gb0):
                            return s.activation(out=fsg[:, 0:icw],
                                                in_=pb[gb0][0:128, 0:icw],
                                                func=AF.Silu)
                        t_si = step("scalar", [tk, t_vh_prev], ssi)

                        def vh(v, icw=icw, gb1=gb1, ic0=ic0):
                            return v.tensor_tensor(
                                out=fha[:, ic0:ic0 + icw],
                                in0=fsg[:, 0:icw],
                                in1=pb[gb1][0:128, 0:icw],
                                op=ALU.mult)
                        t_h = step("vector", [t_si], vh)
                        bank_gu[par] = t_h
                        t_vh_prev = t_h
                        t_h_ch[j] = t_h

                    t_dn_last = None
                    for p in range(2):
                        hs = slice(p * 1408, (p + 1) * 1408)
                        t_hq = quant16(fha[0:128, hs], fhq[0:128, hs],
                                       1408, fhr6, fhsc, fha[0:128, hs],
                                       fzb[0:128, 0:1408],
                                       [t_h_ch[2 + 3 * p], t_hc_prev])
                        hq_tiles = [(fhq[0:128, (11 * p + k) * 128:
                                         (11 * p + k + 1) * 128],
                                     fhqc[0:128, k * 128:(k + 1) * 128],
                                     None) for k in range(11)]
                        t_hc = transp2(hq_tiles, [t_hq])
                        t_hc_prev = t_hc

                        def tdn(t, p=p):
                            for kt4 in range(11):
                                gt = 11 * p + kt4
                                for n in range(2):
                                    t.matmul(
                                        pb[4 + n][0:128, 0:512],
                                        fhqc[:, kt4 * 128:(kt4 + 1) * 128],
                                        w_d[:, gt * DM + n * 512:
                                            gt * DM + (n + 1) * 512],
                                        start=(p == 0 and kt4 == 0),
                                        stop=(p == 1 and kt4 == 10))
                        t_dn_last = step("tensor", [t_hc, bank_d, t_w], tdn)

                    t_dump = None
                    if tt == 0 and os.environ.get("KDW2"):
                        def dtile(g):
                            dma(g, dbg_out[:, 0:1024], fdbg[:], "gpsimd")
                            dma(g, dbg_out[:, 1024:1536], fsg[:], "gpsimd")
                            dma(g, dbg_out[:, 1536:2048], fh[:], "gpsimd")
                        t_dump = step("gpsimd_dma",
                                      [t_dn_last,
                                       ("VQ", nissued["VQ"][0]),
                                       ("SQ", nissued["SQ"][0])],
                                      dtile, n_dma=3)

                    def vres(v, tt=tt):
                        xt = x_sb[:, tt * DM:(tt + 1) * DM]
                        last = None
                        for n in range(2):
                            sl = slice(n * 512, (n + 1) * 512)
                            last = v.scalar_tensor_tensor(
                                out=xt[0:128, sl], in0=xt[0:128, sl],
                                scalar=1.0, in1=pb[4 + n][0:128, 0:512],
                                op0=ALU.mult, op1=ALU.add)
                        return last
                    t_res_prev = step("vector",
                                      [t_dn_last] +
                                      ([t_dump] if t_dump else []), vres)
                    bank_d = t_res_prev

            barrier()
            return t_res_prev

        def emit_attn_b():
            with contextlib.ExitStack() as pbs:
                P = pbs.enter_context
                w_qb = P(nc.sbuf_tensor("w_qb", [128, DKT * DM], BF16))
                w_kb = P(nc.sbuf_tensor("w_kb", [128, DKT * DM], BF16))
                w_vb = P(nc.sbuf_tensor("w_vb", [128, DKT * DM], BF16))
                w_bb = P(nc.sbuf_tensor("w_bb", [128, DKT * 8], BF16))
                s_lw = P(nc.sbuf_tensor("s_lw", [128, DM], F32))
                s_lb = P(nc.sbuf_tensor("s_lb", [128, DM], F32))
                s_mu = P(nc.sbuf_tensor("s_mu", [128, 128], F32))
                s_mls = P(nc.sbuf_tensor("s_mls", [128, 128], F32))
                zbfb = P(nc.sbuf_tensor("zbfb", [128, 4 * DM], BF16))
                zcmb = P(nc.sbuf_tensor("zcmb", [128, DKT * SCT], BF16))
                q_cm = P(nc.sbuf_tensor("q_cm", [128, 8 * SCT], BF16))
                k_cm = P(nc.sbuf_tensor("k_cm", [128, 8 * SCT], BF16))
                khcm = P(nc.sbuf_tensor("khcm", [128, 8 * SCT], BF16))
                v_tk = P(nc.sbuf_tensor("v_tk", [128, 4 * DM], BF16))
                k_tk = P(nc.sbuf_tensor("k_tk", [128, 4 * DM], BF16))
                bb_t = P(nc.sbuf_tensor("bb_t", [128, 32], F32))
                bb_n = P(nc.sbuf_tensor("bb_n", [128, 32], F32))
                rn_b = P(nc.sbuf_tensor("rn_b", [128, 32], F32))
                m_al = P(nc.sbuf_tensor("m_al", [128, 8 * 256], BF16))
                p_s = P(nc.sbuf_tensor("p_s", [128, 8 * 128], BF16))
                pt_s = P(nc.sbuf_tensor("pt_s", [128, 8 * 128], BF16))
                pti_s = P(nc.sbuf_tensor("pti_s", [128, 8 * 128], BF16))
                asct = P(nc.sbuf_tensor("asct", [128, 8 * 128], BF16))
                nwp = P(nc.sbuf_tensor("nwp", [128, 8 * 128], BF16))
                w_cmb = P(nc.sbuf_tensor("w_cmb", [128, 8 * 128], BF16))
                s0_s = P(nc.sbuf_tensor("s0_s", [128, 8 * 128], BF16))
                pst = P(nc.sbuf_tensor("pst", [128, 8 * 128], BF16))
                o0_b = P(nc.sbuf_tensor("o0_b", [128, DM], BF16))
                gcmb = P(nc.sbuf_tensor("gcmb", [128, DM], BF16))
                ftmp = P(nc.sbuf_tensor("ftmp", [128, DM], F32))
                ftmp2 = P(nc.sbuf_tensor("ftmp2", [128, DM], F32))
                sin_s = P(nc.sbuf_tensor("sin_s", [128, 8 * 128], BF16))
                agld = P(nc.sbuf_tensor("agld", [128, 2048], BF16))
                pjt = P(nc.sbuf_tensor("pjt", [128, 8 * 128], BF16))
                qeff = P(nc.sbuf_tensor("qeff", [128, 8 * 128], BF16))
                o0ld = P(nc.sbuf_tensor("o0ld", [128, 2 * DM], BF16))
                gld = P(nc.sbuf_tensor("gld", [128, 2 * DM], BF16))

                def ldc(sy):
                    dma(sy, s_lw[:], lnwx)
                    dma(sy, s_lb[:], lnbx)
                    dma(sy, s_mu[:], mu128)
                    dma(sy, s_mls[:], ml128s)
                t_c = step("sync", [], ldc, n_dma=4)

                def ldw(g):
                    for kt in range(DKT):
                        dma(g, w_qb[:, kt * DM:(kt + 1) * DM],
                            wqb[128 * kt:128 * (kt + 1), :], "gpsimd")
                        dma(g, w_kb[:, kt * DM:(kt + 1) * DM],
                            wkb[128 * kt:128 * (kt + 1), :], "gpsimd")
                        dma(g, w_vb[:, kt * DM:(kt + 1) * DM],
                            wvb[128 * kt:128 * (kt + 1), :], "gpsimd")
                        dma(g, w_bb[:, kt * 8:(kt + 1) * 8],
                            wbb[128 * kt:128 * (kt + 1), :], "gpsimd")
                    pass
                t_w = step("gpsimd_dma", [], ldw, n_dma=4 * DKT)

                t_st = ln_stats(x_sb, st_nmu, st_rst, ftmp,
                                [("VQ", nissued["VQ"][0]),
                                 ("SQ", nissued["SQ"][0])])

                def v_ini(v):
                    v.memset(s0_s[:], 0.0)
                    return v.tensor_copy(
                        out=pst[:].rearrange("p (h d) -> p h d", d=128),
                        in_=s_id[:].unsqueeze(1).broadcast_to([128, 8, 128]))
                t_ini = step("vector", [t_c], v_ini)

                t_s0p = t_ini
                t_pp = t_ini
                t_mb_prev = t_ini
                t_su_last = t_ini
                t_g_last = t_ini
                bankt = {i: ("SQ", 0) for i in range(6)}
                t_carry = [t_ini]

                for u in range(SC):
                    def vz(v, u=u):
                        last = None
                        for i in range(4):
                            t = 4 * u + i
                            last = v.tensor_scalar(
                                out=zbfb[:, i * DM:(i + 1) * DM],
                                in0=x_sb[:, t * DM:(t + 1) * DM],
                                scalar1=st_nmu[:, t:t + 1],
                                scalar2=st_rst[:, t:t + 1],
                                op0=ALU.add, op1=ALU.mult)
                        return last
                    t_z = step("vector", [t_st] + t_carry, vz)

                    ztiles = [(zbfb[0:128, i * DM + kt * 128:
                                    i * DM + (kt + 1) * 128],
                               zcmb[0:128, kt * SCT + i * 128:
                                    kt * SCT + (i + 1) * 128],
                               None) for i in range(4) for kt in range(DKT)]
                    t_zcm = transp2(ztiles, [t_z] + t_carry)

                    pev = {}
                    for h in range(8):
                        def tpq(t, h=h):
                            for kt in range(DKT):
                                t.matmul(pb[0][0:128, 0:512],
                                         w_qb[:, kt * DM + h * 128:
                                              kt * DM + (h + 1) * 128],
                                         zcmb[:, kt * SCT:(kt + 1) * SCT],
                                         start=(kt == 0),
                                         stop=(kt == DKT - 1))
                        tk = step("tensor", [t_zcm, bankt[0], t_w], tpq)
                        ev = step("scalar", [tk], lambda s, h=h: s.copy(
                            out=q_cm[:, h * SCT:(h + 1) * SCT],
                            in_=pb[0][0:128, 0:512]))
                        bankt[0] = ev
                        pev[("q", h)] = ev
                    for h in range(8):
                        def tpk(t, h=h):
                            for kt in range(DKT):
                                t.matmul(pb[1][0:128, 0:512],
                                         w_kb[:, kt * DM + h * 128:
                                              kt * DM + (h + 1) * 128],
                                         zcmb[:, kt * SCT:(kt + 1) * SCT],
                                         start=(kt == 0),
                                         stop=(kt == DKT - 1))
                        tk = step("tensor", [t_zcm, bankt[1], t_w], tpk)
                        ev = step("scalar", [tk], lambda s, h=h: s.copy(
                            out=k_cm[:, h * SCT:(h + 1) * SCT],
                            in_=pb[1][0:128, 0:512]))
                        bankt[1] = ev
                        pev[("k", h)] = ev
                    for i in range(4):
                        for n in range(2):
                            def tpv(t, i=i, n=n):
                                for kt in range(DKT):
                                    t.matmul(pb[2 + n][0:128, 0:512],
                                             zcmb[:, kt * SCT + i * 128:
                                                  kt * SCT + (i + 1) * 128],
                                             w_vb[:, kt * DM + n * 512:
                                                  kt * DM + (n + 1) * 512],
                                             start=(kt == 0),
                                             stop=(kt == DKT - 1))
                            tk = step("tensor", [t_zcm, bankt[2 + n], t_w],
                                      tpv)
                            ev = step("scalar", [tk],
                                      lambda s, i=i, n=n: s.activation(
                                          out=v_tk[:, i * DM + n * 512:
                                                   i * DM + (n + 1) * 512],
                                          in_=pb[2 + n][0:128, 0:512],
                                          func=AF.Silu))
                            bankt[2 + n] = ev
                            pev[("v", i, n)] = ev
                    for i in range(4):
                        def tpb(t, i=i):
                            for kt in range(DKT):
                                t.matmul(pb[4][0:128, 0:8],
                                         zcmb[:, kt * SCT + i * 128:
                                              kt * SCT + (i + 1) * 128],
                                         w_bb[:, kt * 8:(kt + 1) * 8],
                                         start=(kt == 0),
                                         stop=(kt == DKT - 1))
                        tk = step("tensor", [t_zcm, bankt[4], t_w], tpb)
                        ev = step("scalar", [tk], lambda s, i=i: s.activation(
                            out=bb_t[:, i * 8:(i + 1) * 8],
                            in_=pb[4][0:128, 0:8], func=AF.Sigmoid))
                        bankt[4] = ev
                        pev[("b", i)] = ev
                    t_bb = step("vector", [pev[("b", i)] for i in range(4)],
                                lambda v: v.tensor_scalar(
                                    out=bb_n[:], in0=bb_t[:], scalar1=0.0,
                                    scalar2=-1.0, op0=ALU.add, op1=ALU.mult))

                    ktile = [(k_cm[0:128, h * SCT + i * 128:
                                   h * SCT + (i + 1) * 128],
                              k_tk[0:128, i * DM + h * 128:
                                   i * DM + (h + 1) * 128],
                              None) for i in range(4) for h in range(8)]
                    t_ktk = transp2(ktile, [pev[("k", h)] for h in range(8)])

                    def v_rn(v):
                        v.tensor_tensor(out=ftmp[:], in0=k_tk[:, 0:DM],
                                        in1=k_tk[:, 0:DM], op=ALU.mult)
                        v.tensor_reduce(
                            out=rn_b[:, 0:8],
                            in_=ftmp[:].rearrange("p (h d) -> p h d", d=128),
                            axis=AX, op=ALU.add)
                        for i in range(1, 4):
                            v.tensor_tensor(out=ftmp[:],
                                            in0=k_tk[:, i * DM:(i + 1) * DM],
                                            in1=k_tk[:, i * DM:(i + 1) * DM],
                                            op=ALU.mult)
                            v.tensor_reduce(
                                out=rn_b[:, i * 8:(i + 1) * 8],
                                in_=ftmp[:].rearrange("p (h d) -> p h d",
                                                      d=128),
                                axis=AX, op=ALU.add)
                        return v.tensor_scalar(out=rn_b[:], in0=rn_b[:],
                                               scalar1=1e-24, scalar2=None,
                                               op0=ALU.max)
                    t_k2 = step("vector", [t_ktk, t_mb_prev], v_rn)
                    t_rn = step("scalar", [t_k2], lambda s: (
                        s.activation(out=rn_b[:], in_=rn_b[:], func=AF.Ln),
                        s.activation(out=rn_b[:], in_=rn_b[:], func=AF.Exp,
                                     scale=-0.5))[-1])

                    def v_kn(v):
                        return v.tensor_tensor(
                            out=k_tk[:].rearrange("p (g d) -> p g d", d=128),
                            in0=k_tk[:].rearrange("p (g d) -> p g d", d=128),
                            in1=rn_b[:].unsqueeze(2).broadcast_to(
                                [128, 32, 128]),
                            op=ALU.mult)
                    t_kn = step("vector", [t_rn], v_kn)

                    khtile = [(k_tk[0:128, i * DM + h * 128:
                                    i * DM + (h + 1) * 128],
                               khcm[0:128, h * SCT + i * 128:
                                    h * SCT + (i + 1) * 128],
                               None) for i in range(4) for h in range(8)]
                    t_kh = transp2(khtile, [t_kn])

                    for j in range(4):
                        jj = 4 * u + j

                        def t_arf(t, j=j):
                            for h in range(8):
                                t.matmul(pb[2 + h // 4][0:128,
                                                        (h % 4) * 128:
                                                        (h % 4) * 128 + 128],
                                         khcm[:, h * SCT + j * 128:
                                              h * SCT + (j + 1) * 128],
                                         khcm[:, h * SCT + j * 128:
                                              h * SCT + (j + 1) * 128],
                                         start=True, stop=True)
                            for h in range(8):
                                t.matmul(pb[h // 4][0:128,
                                                    (h % 4) * 128:
                                                    (h % 4) * 128 + 128],
                                         khcm[:, h * SCT + j * 128:
                                              h * SCT + (j + 1) * 128],
                                         q_cm[:, h * SCT + j * 128:
                                              h * SCT + (j + 1) * 128],
                                         start=True, stop=True)
                        t_ar = step("tensor", [t_kh, pev[("q", 7)],
                                               bankt[0], bankt[1],
                                               bankt[2], bankt[3]],
                                    t_arf)

                        def s_na(s, j=j):
                            for h in range(8):
                                s.mul(
                                    out=p_s[:, h * 128:(h + 1) * 128],
                                    in_=pb[2 + h // 4][0:128,
                                                       (h % 4) * 128:
                                                       (h % 4) * 128 + 128],
                                    mul=bb_n[:, j * 8 + h:j * 8 + h + 1])
                            s.copy(out=asct[:, 0:512],
                                   in_=pb[0][0:128, 0:512])
                            return s.copy(out=asct[:, 512:1024],
                                          in_=pb[1][0:128, 0:512])
                        t_na = step("scalar", [t_ar, t_bb], s_na)
                        bankt[0] = t_na
                        bankt[1] = t_na
                        bankt[2] = t_na
                        bankt[3] = t_na

                        def v_msk(v):
                            v.tensor_tensor(
                                out=p_s[:].rearrange("p (h c) -> p h c",
                                                     c=128),
                                in0=p_s[:].rearrange("p (h c) -> p h c",
                                                     c=128),
                                in1=s_mls[:].unsqueeze(1).broadcast_to(
                                    [128, 8, 128]),
                                op=ALU.mult)
                            return v.tensor_tensor(
                                out=asct[:].rearrange("p (h c) -> p h c",
                                                      c=128),
                                in0=asct[:].rearrange("p (h c) -> p h c",
                                                      c=128),
                                in1=s_mu[:].unsqueeze(1).broadcast_to(
                                    [128, 8, 128]),
                                op=ALU.mult)
                        t_msk = step("vector", [t_na], v_msk)
                        t_astm = t_msk

                        attile = [(p_s[0:128, h * 128:(h + 1) * 128],
                                   pt_s[0:128, h * 128:(h + 1) * 128],
                                   None) for h in range(8)]
                        t_at = transp2(attile, [t_msk])

                        def v_mb(v, j=j):
                            v.tensor_tensor(
                                out=m_al[:].rearrange(
                                    "p (h c) -> p h c", c=256)[:, :, 0:128],
                                in0=k_tk[:, j * DM:(j + 1) * DM].rearrange(
                                    "p (h d) -> p h d", d=128),
                                in1=bb_t[:, j * 8:(j + 1) * 8].unsqueeze(2)
                                .broadcast_to([128, 8, 128]),
                                op=ALU.mult)
                            return v.tensor_tensor(
                                out=m_al[:].rearrange(
                                    "p (h c) -> p h c", c=256)[:, :, 128:256],
                                in0=v_tk[:, j * DM:(j + 1) * DM].rearrange(
                                    "p (h d) -> p h d", d=128),
                                in1=bb_t[:, j * 8:(j + 1) * 8].unsqueeze(2)
                                .broadcast_to([128, 8, 128]),
                                op=ALU.mult)
                        t_mb = step("vector", [t_kn, t_bb, t_su_last], v_mb)
                        t_mb_prev = t_mb

                        tw_e = {0: t_at, 1: t_at}
                        tw_t = {0: t_at, 1: t_at}
                        for it in range(7):
                            for w in range(2):
                                def t_serf(t, it=it, w=w):
                                    for q in range(4):
                                        h = 4 * w + q
                                        hs = slice(h * 128, (h + 1) * 128)
                                        if it <= 5:
                                            t.matmul(
                                                pb[3 * w][0:128,
                                                          q * 128:
                                                          q * 128 + 128],
                                                pt_s[:, hs], p_s[:, hs],
                                                start=True, stop=True)
                                        t.matmul(
                                            pb[3 * w + 1 + q // 2][
                                                0:128,
                                                (q % 2) * 256:
                                                (q % 2) * 256 + 256],
                                            pt_s[:, hs],
                                            m_al[:, h * 256:(h + 1) * 256],
                                            start=True, stop=False)
                                        t.matmul(
                                            pb[3 * w + 1 + q // 2][
                                                0:128,
                                                (q % 2) * 256:
                                                (q % 2) * 256 + 256],
                                            s_id[:],
                                            m_al[:, h * 256:(h + 1) * 256],
                                            start=False, stop=True)
                                t_sw = step("tensor",
                                            [t_mb, tw_e[w], tw_t[w],
                                             bankt[3 * w],
                                             bankt[3 * w + 1],
                                             bankt[3 * w + 2]], t_serf)

                                def s_sev(s, it=it, w=w):
                                    ws = slice(4 * w * 128,
                                               (4 * w + 4) * 128)
                                    if it <= 5:
                                        s.copy(out=p_s[:, ws],
                                               in_=pb[3 * w][0:128, 0:512])
                                    s.copy(out=m_al[:, 4 * w * 256:
                                                    (4 * w + 2) * 256],
                                           in_=pb[3 * w + 1][0:128, 0:512])
                                    return s.copy(
                                        out=m_al[:, (4 * w + 2) * 256:
                                                 (4 * w + 4) * 256],
                                        in_=pb[3 * w + 2][0:128, 0:512])
                                t_se = step("scalar", [t_sw], s_sev)
                                tw_e[w] = t_se
                                bankt[3 * w] = t_se
                                bankt[3 * w + 1] = t_se
                                bankt[3 * w + 2] = t_se
                                if it <= 5:
                                    sttile = [
                                        (p_s[0:128,
                                             (4 * w + q) * 128:
                                             (4 * w + q + 1) * 128],
                                         pt_s[0:128,
                                              (4 * w + q) * 128:
                                              (4 * w + q + 1) * 128],
                                         None) for q in range(4)]
                                    tw_t[w] = transp2(sttile, [t_se])
                        t_ser = [tw_e[0], tw_e[1]]

                        wtile = [(m_al[0:128, h * 256:h * 256 + 128],
                                  w_cmb[0:128, h * 128:(h + 1) * 128],
                                  None) for h in range(8)]
                        t_wt = transp2(wtile, t_ser)

                        def t_wsf(t):
                            for h in range(8):
                                t.matmul(pb[h // 4][0:128,
                                                    (h % 4) * 128:
                                                    (h % 4) * 128 + 128],
                                         w_cmb[:, h * 128:(h + 1) * 128],
                                         s0_s[:, h * 128:(h + 1) * 128],
                                         start=True, stop=True)
                        t_ws = step("tensor", [t_wt, t_s0p, bankt[0],
                                               bankt[1]], t_wsf)
                        t_wse = step("scalar", [t_ws], lambda s: (
                            s.mul(out=ftmp[:, 0:512],
                                  in_=pb[0][0:128, 0:512], mul=-1.0),
                            s.mul(out=ftmp[:, 512:1024],
                                  in_=pb[1][0:128, 0:512], mul=-1.0))[-1])
                        bankt[0] = t_wse
                        bankt[1] = t_wse

                        def v_uc(v):
                            return v.tensor_tensor(
                                out=m_al[:].rearrange(
                                    "p (h c) -> p h c", c=256)[:, :, 128:256],
                                in0=m_al[:].rearrange(
                                    "p (h c) -> p h c", c=256)[:, :, 128:256],
                                in1=ftmp[:].rearrange("p (h d) -> p h d",
                                                      d=128),
                                op=ALU.add)
                        t_uc = step("vector", [t_wse], v_uc)

                        def t_o0f(t, j=j):
                            for h in range(8):
                                bk = pb[h // 4]
                                cs = slice((h % 4) * 128, (h % 4) * 128 + 128)
                                t.matmul(bk[0:128, cs],
                                         q_cm[:, h * SCT + j * 128:
                                              h * SCT + (j + 1) * 128],
                                         s0_s[:, h * 128:(h + 1) * 128],
                                         start=True, stop=False)
                                t.matmul(bk[0:128, cs],
                                         asct[:, h * 128:(h + 1) * 128],
                                         m_al[:, h * 256 + 128:
                                              h * 256 + 256],
                                         start=False, stop=True)
                        t_o0 = step("tensor", [t_uc, t_astm, t_s0p,
                                               bankt[0], bankt[1]],
                                    t_o0f)
                        t_o0e = step("scalar", [t_o0], lambda s: (
                            s.copy(out=o0_b[:, 0:512],
                                   in_=pb[0][0:128, 0:512]),
                            s.copy(out=o0_b[:, 512:1024],
                                   in_=pb[1][0:128, 0:512]))[-1])
                        bankt[0] = t_o0e
                        bankt[1] = t_o0e
                        t_o0d = step("sync", [t_o0e], lambda sy, jj=jj: dma(
                            sy, o0scr.ap()[:, jj * DM:(jj + 1) * DM],
                            o0_b[:]), n_dma=1)

                        def t_wpf(t):
                            for h in range(8):
                                t.matmul(pb[2 + h // 4][0:128,
                                                        (h % 4) * 128:
                                                        (h % 4) * 128 + 128],
                                         w_cmb[:, h * 128:(h + 1) * 128],
                                         pst[:, h * 128:(h + 1) * 128],
                                         start=True, stop=True)
                        t_wp = step("tensor", [t_wt, t_pp, bankt[2],
                                               bankt[3]], t_wpf)
                        t_wpe = step("scalar", [t_wp], lambda s: (
                            s.mul(out=nwp[:, 0:512],
                                  in_=pb[2][0:128, 0:512], mul=-1.0),
                            s.mul(out=nwp[:, 512:1024],
                                  in_=pb[3][0:128, 0:512], mul=-1.0))[-1])
                        bankt[2] = t_wpe
                        bankt[3] = t_wpe

                        def t_gf(t, j=j):
                            for h in range(8):
                                bk = pb[4 + h // 4]
                                cs = slice((h % 4) * 128, (h % 4) * 128 + 128)
                                t.matmul(bk[0:128, cs],
                                         pst[:, h * 128:(h + 1) * 128],
                                         q_cm[:, h * SCT + j * 128:
                                              h * SCT + (j + 1) * 128],
                                         start=True, stop=False)
                                t.matmul(bk[0:128, cs],
                                         nwp[:, h * 128:(h + 1) * 128],
                                         asct[:, h * 128:(h + 1) * 128],
                                         start=False, stop=True)
                        t_g = step("tensor", [t_wpe, t_astm, t_pp,
                                              bankt[4], bankt[5]], t_gf)
                        t_ge = step("scalar", [t_g], lambda s: (
                            s.copy(out=gcmb[:, 0:512],
                                   in_=pb[4][0:128, 0:512]),
                            s.copy(out=gcmb[:, 512:1024],
                                   in_=pb[5][0:128, 0:512]))[-1])
                        bankt[4] = t_ge
                        bankt[5] = t_ge
                        t_g_last = t_ge
                        step("sync", [t_ge], lambda sy, jj=jj: dma(
                            sy, gscr.ap()[:, jj * DM:(jj + 1) * DM],
                            gcmb[:]), n_dma=1)

                        def t_puf(t, j=j):
                            for h in range(8):
                                bk = pb[2 + h // 4]
                                cs = slice((h % 4) * 128, (h % 4) * 128 + 128)
                                t.matmul(bk[0:128, cs],
                                         k_tk[:, j * DM + h * 128:
                                              j * DM + (h + 1) * 128],
                                         nwp[:, h * 128:(h + 1) * 128],
                                         start=True, stop=False)
                                t.matmul(bk[0:128, cs], s_id[:],
                                         pst[:, h * 128:(h + 1) * 128],
                                         start=False, stop=True)
                        t_pu = step("tensor", [t_wpe, bankt[2], bankt[3],
                                               t_g], t_puf)

                        def t_suf(t, j=j):
                            for h in range(8):
                                bk = pb[h // 4]
                                cs = slice((h % 4) * 128, (h % 4) * 128 + 128)
                                t.matmul(bk[0:128, cs],
                                         k_tk[:, j * DM + h * 128:
                                              j * DM + (h + 1) * 128],
                                         m_al[:, h * 256 + 128:
                                              h * 256 + 256],
                                         start=True, stop=False)
                                t.matmul(bk[0:128, cs], s_id[:],
                                         s0_s[:, h * 128:(h + 1) * 128],
                                         start=False, stop=True)
                        t_su = step("tensor", [t_pu, t_o0, bankt[0],
                                               bankt[1]], t_suf)
                        t_su_last = t_su
                        t_sue = step("scalar", [t_su], lambda s: (
                            s.copy(out=pst[:, 0:512],
                                   in_=pb[2][0:128, 0:512]),
                            s.copy(out=pst[:, 512:1024],
                                   in_=pb[3][0:128, 0:512]),
                            s.copy(out=s0_s[:, 0:512],
                                   in_=pb[0][0:128, 0:512]),
                            s.copy(out=s0_s[:, 512:1024],
                                   in_=pb[1][0:128, 0:512]))[-1])
                        bankt[0] = t_sue
                        bankt[1] = t_sue
                        bankt[2] = t_sue
                        bankt[3] = t_sue
                        t_s0p = t_sue
                        t_pev = t_sue
                        t_pp = t_sue

                    t_carry = [t_g_last, t_sue, t_pev, t_mb]

                def agw(sy):
                    dma(sy, ag2_i.ap()[:, 0:1024], pst[:])
                    dma(sy, ag2_i.ap()[:, 1024:2048], s0_s[:])
                t_agw = step("sync", [t_pp, t_s0p], agw, n_dma=2)

                def dcc2(g):
                    g.collective_compute(
                        "AllGather", ALU.bypass,
                        replica_groups=[list(range(8))],
                        ins=[ag2_i.ap()], outs=[ag2_o.ap()],
                    ).then_inc(sems["CC"][0], 1)
                step("gpsimd", [t_agw, ("GD", sems["GD"][1])], dcc2)
                sems["CC"][1] += 1
                t_ag2 = ("CC", sems["CC"][1])

                t_z0 = step("vector", [t_ag2],
                            lambda v: v.memset(sin_s[:], 0.0))
                t_fold = t_z0
                for jr in range(8):
                    t_lj = step("sync", [t_fold], lambda sy, jr=jr: dma(
                        sy, agld[:],
                        ag2_o.ap()[128 * jr:128 * (jr + 1), :]), n_dma=1)
                    pjtile = [(agld[0:128, h * 128:(h + 1) * 128],
                               pjt[0:128, h * 128:(h + 1) * 128],
                               None) for h in range(8)]
                    t_pjt = transp2(pjtile, [t_lj, t_fold])

                    def v_me(v, jr=jr):
                        v.tensor_scalar(out=pjt[:], in0=pjt[:],
                                        scalar1=0.0,
                                        scalar2=s_pm[:, jr:jr + 1],
                                        op0=ALU.add, op1=ALU.mult)
                        v.scalar_tensor_tensor(
                            out=pjt[:].rearrange("p (h c) -> p h c", c=128),
                            in0=s_id[:].unsqueeze(1).broadcast_to(
                                [128, 8, 128]),
                            scalar=s_pmi[:, jr:jr + 1],
                            in1=pjt[:].rearrange("p (h c) -> p h c", c=128),
                            op0=ALU.mult, op1=ALU.add)
                        return v.tensor_scalar(
                            out=qeff[:], in0=agld[:, 1024:2048],
                            scalar1=0.0, scalar2=s_pm[:, jr:jr + 1],
                            op0=ALU.add, op1=ALU.mult)
                    t_me = step("vector", [t_pjt], v_me)

                    def t_fmf(t):
                        for h in range(8):
                            bk = pb[h // 4]
                            cs = slice((h % 4) * 128, (h % 4) * 128 + 128)
                            t.matmul(bk[0:128, cs],
                                     pjt[:, h * 128:(h + 1) * 128],
                                     sin_s[:, h * 128:(h + 1) * 128],
                                     start=True, stop=False)
                            t.matmul(bk[0:128, cs], s_id[:],
                                     qeff[:, h * 128:(h + 1) * 128],
                                     start=False, stop=True)
                    t_fm = step("tensor", [t_me, bankt[0], bankt[1]], t_fmf)
                    t_fe = step("scalar", [t_fm], lambda s: (
                        s.copy(out=sin_s[:, 0:512],
                               in_=pb[0][0:128, 0:512]),
                        s.copy(out=sin_s[:, 512:1024],
                               in_=pb[1][0:128, 0:512]))[-1])
                    bankt[0] = t_fe
                    bankt[1] = t_fe
                    t_fold = t_fe

                t_ld = ("SD", sems["SD"][1])
                t_app = t_fold
                t_prev_app = {0: t_fold, 1: t_fold}
                for jj in range(TT):
                    pr = jj % 2
                    b0, b1 = (0, 1) if pr == 0 else (2, 3)

                    def ldo(sy, jj=jj, pr=pr):
                        dma(sy, o0ld[:, pr * DM:(pr + 1) * DM],
                            o0scr.ap()[:, jj * DM:(jj + 1) * DM])
                        dma(sy, gld[:, pr * DM:(pr + 1) * DM],
                            gscr.ap()[:, jj * DM:(jj + 1) * DM])
                    t_lo = step("sync", [t_prev_app[pr], t_ld], ldo, n_dma=2)
                    t_ld = t_lo

                    def t_ocf(t, pr=pr, b0=b0, b1=b1):
                        for h in range(8):
                            bk = pb[b0 + h // 4]
                            cs = slice((h % 4) * 128, (h % 4) * 128 + 128)
                            t.matmul(bk[0:128, cs],
                                     gld[:, pr * DM + h * 128:
                                         pr * DM + (h + 1) * 128],
                                     sin_s[:, h * 128:(h + 1) * 128],
                                     start=True, stop=False)
                            t.matmul(bk[0:128, cs], s_id[:],
                                     o0ld[:, pr * DM + h * 128:
                                          pr * DM + (h + 1) * 128],
                                     start=False, stop=True)
                    t_oc = step("tensor", [t_lo, t_fold, bankt[b0],
                                           bankt[b1]], t_ocf)

                    nmu_c = st_nm2[:, jj:jj + 1]
                    rs_c = st_rs2[:, jj:jj + 1]

                    fx = ftmp if pr == 0 else ftmp2
                    t_sq = step("scalar", [t_oc, t_prev_app[pr]],
                                lambda s, b0=b0, b1=b1, fx=fx: (
                        s.activation(out=fx[:, 0:512],
                                     in_=pb[b0][0:128, 0:512],
                                     func=AF.Square),
                        s.activation(out=fx[:, 512:1024],
                                     in_=pb[b1][0:128, 0:512],
                                     func=AF.Square))[-1])

                    def v_st1(v, jj=jj, b0=b0, b1=b1, fx=fx):
                        v.tensor_reduce(out=st_t1[:, 0:1],
                                        in_=pb[b0][0:128, 0:512], axis=AX,
                                        op=ALU.add)
                        v.tensor_reduce(out=st_t1[:, 1:2],
                                        in_=pb[b1][0:128, 0:512], axis=AX,
                                        op=ALU.add)
                        v.tensor_tensor(out=st_t1[:, 0:1],
                                        in0=st_t1[:, 0:1],
                                        in1=st_t1[:, 1:2], op=ALU.add)
                        v.tensor_scalar(out=st_nm2[:, jj:jj + 1],
                                        in0=st_t1[:, 0:1], scalar1=0.0,
                                        scalar2=-1.0 / DM, op0=ALU.add,
                                        op1=ALU.mult)
                        v.tensor_reduce(out=st_t1[:, 2:3],
                                        in_=fx[:, 0:512], axis=AX,
                                        op=ALU.add)
                        v.tensor_reduce(out=st_t1[:, 3:4],
                                        in_=fx[:, 512:1024], axis=AX,
                                        op=ALU.add)
                        v.tensor_tensor(out=st_t1[:, 2:3],
                                        in0=st_t1[:, 2:3],
                                        in1=st_t1[:, 3:4], op=ALU.add)
                        v.tensor_scalar(out=st_t1[:, 2:3],
                                        in0=st_t1[:, 2:3], scalar1=0.0,
                                        scalar2=1.0 / DM, op0=ALU.add,
                                        op1=ALU.mult)
                        v.tensor_tensor(out=st_t1[:, 1:2],
                                        in0=st_nm2[:, jj:jj + 1],
                                        in1=st_nm2[:, jj:jj + 1],
                                        op=ALU.mult)
                        v.tensor_tensor(out=st_rs2[:, jj:jj + 1],
                                        in0=st_t1[:, 2:3],
                                        in1=st_t1[:, 1:2], op=ALU.subtract)
                        return v.tensor_scalar(out=st_rs2[:, jj:jj + 1],
                                               in0=st_rs2[:, jj:jj + 1],
                                               scalar1=LNEPS, scalar2=None,
                                               op0=ALU.add)
                    t_st1 = step("vector", [t_oc, t_sq, t_app], v_st1)
                    t_rsq = step("scalar", [t_st1], lambda s, rs_c=rs_c: (
                        s.activation(out=rs_c, in_=rs_c, func=AF.Ln),
                        s.activation(out=rs_c, in_=rs_c, func=AF.Exp,
                                     scale=-0.5))[-1])

                    def v_app(v, jj=jj, nmu_c=nmu_c, rs_c=rs_c,
                              b0=b0, b1=b1, fx=fx):
                        for n, bk in enumerate((b0, b1)):
                            v.tensor_scalar(
                                out=fx[:, n * 512:(n + 1) * 512],
                                in0=pb[bk][0:128, 0:512],
                                scalar1=nmu_c, scalar2=rs_c,
                                op0=ALU.add, op1=ALU.mult)
                        v.tensor_tensor(out=fx[:], in0=fx[:],
                                        in1=s_lw[:], op=ALU.mult)
                        v.tensor_tensor(out=fx[:], in0=fx[:],
                                        in1=s_lb[:], op=ALU.add)
                        xt = x_sb[:, jj * DM:(jj + 1) * DM]
                        return v.scalar_tensor_tensor(
                            out=xt, in0=fx[:], scalar=1.0, in1=xt,
                            op0=ALU.mult, op1=ALU.add)
                    t_app = step("vector", [t_rsq], v_app)
                    t_prev_app[pr] = t_app
                    bankt[b0] = t_app
                    bankt[b1] = t_app

            barrier()

        emit_ffn()
        if STAGE == 2:
            emit_epilogue(dbg=DBG_APS)
            flush_prog()
            return nc

        emit_dnb_done = False
        if STAGE >= 3:
            emit_dnb_done = True
            emit_attn_b()

        if STAGE >= 4:
            emit_ffn("2")

        emit_epilogue(dbg=DBG_APS)
        flush_prog()
    return nc


_CACHE = {}


def kernel(**inputs):
    x = np.ascontiguousarray(inputs["L0_attn_a_x"], np.float32)
    B, T, D = x.shape
    spans = []
    for c in range(8):
        b, s = c // 4, c % 4
        spans.append(np.ascontiguousarray(x[b, s * NT:(s + 1) * NT]))

    f32 = lambda a: np.ascontiguousarray(a, np.float32)
    ag = lambda a: np.ascontiguousarray(a)

    triu = np.triu(np.ones((64, 64), np.float32))
    tril1 = np.tril(np.ones((64, 64), np.float32), -1)
    rep = lambda m: np.vstack([m, m]).astype(np.float32)
    half = int(np.float32(0.5).view(np.uint32))
    icon = np.tile(np.array([[0x1FFFFF, 0xFFC00000 - (1 << 32), -(1 << 31),
                              0x7FFFFFFF, half, 0, 0, 0]], np.int64)
                   .astype(np.int32), (128, 1))
    scanm_v = np.ones((128, SCT), np.float32)
    scanm_v[:, ::CH] = 0.0
    gam = f32(inputs["L0_attn_a_gamma"]).reshape(1, DM)
    bet = f32(inputs["L0_attn_a_beta"]).reshape(1, DM)
    lnw = f32(inputs["L0_attn_b_ln_w"]).reshape(1, DM)
    lnb = f32(inputs["L0_attn_b_ln_b"]).reshape(1, DM)
    ba = f32(inputs["L0_attn_a_ba"]).reshape(4, 128).T

    base = dict(
        wqa=f32(inputs["L0_attn_a_Wq"]), wka=f32(inputs["L0_attn_a_Wk"]),
        wva=f32(inputs["L0_attn_a_Wv"]), waa=f32(inputs["L0_attn_a_Wa"]),
        baa=ag(ba), woa=f32(inputs["L0_attn_a_Wo"]),
        gamx=np.broadcast_to(gam, (128, DM)).copy(),
        betx=np.broadcast_to(bet, (128, DM)).copy(),
        wqb=f32(inputs["L0_attn_b_Wq"]), wkb=f32(inputs["L0_attn_b_Wk"]),
        wvb=f32(inputs["L0_attn_b_Wv"]), wbb=f32(inputs["L0_attn_b_Wb"]),
        lnwx=np.broadcast_to(lnw, (128, DM)).copy(),
        lnbx=np.broadcast_to(lnb, (128, DM)).copy(),
        ident=np.eye(128, dtype=BF), i64b=rep(np.eye(64)).astype(BF),
        m_triu=rep(triu), m_tril1=rep(tril1),
        mu128=np.triu(np.ones((128, 128), np.float32)),
        ml128s=np.tril(np.ones((128, 128), np.float32), -1),
        scanm=scanm_v, ones8=np.ones((128, 8), np.float32), iconst=icon,
    )
    wg = f32(inputs["L0_ffn_w_gate"])
    wu = f32(inputs["L0_ffn_w_up"])
    wd = f32(inputs["L0_ffn_w_down"])

    key = "nc"
    if key not in _CACHE:
        _CACHE[key] = build()
    nc = _CACHE[key]

    in_maps = []
    for c in range(8):
        b, s = c // 4, c % 4
        pmv = np.zeros((128, 8), np.float32)
        for j in range(4 * b, 4 * b + s):
            pmv[:, j] = 1.0
        d = dict(base)
        d["x_in"] = spans[c]
        d["pm"] = pmv
        d["pmi"] = 1.0 - pmv
        d["wg_sl"] = np.ascontiguousarray(wg[:, 128 * c:128 * (c + 1)])
        d["wu_sl"] = np.ascontiguousarray(wu[:, 128 * c:128 * (c + 1)])
        wds = [wd[:, 128 * t:128 * (t + 1)] for t in
               (c, c + 8, min(c + 16, 21))]
        d["wd_sl"] = np.ascontiguousarray(np.concatenate(wds, axis=1))
        in_maps.append(d)

    res = run_bass_kernel_spmd(nc, in_maps, core_ids=list(range(8)))
    out = np.empty((B, T, D), np.float32)
    for c in range(8):
        b, s = c // 4, c % 4
        out[b, s * NT:(s + 1) * NT] = res.results[c]["x_out"]
    return out
